# version 08
# baseline (speedup 1.0000x reference)
import numpy as np
import ml_dtypes
import concourse.bass as bass
import concourse.mybir as mybir
from concourse.bass_utils import run_bass_kernel_spmd

F32 = mybir.dt.float32
BF16 = mybir.dt.bfloat16
I8 = mybir.dt.int8
AF = mybir.ActivationFunctionType
ALU = mybir.AluOpType
BF = ml_dtypes.bfloat16

D = 2048
T = 4096
NSUP = 8
SUP = 512
NDT = 16
HD = 512
INTER = 8192
NHEAD = 16
NKV = 4
DHEAD = 128
EPS_LN = 1e-5
EPS_RMS = 1e-6
SOFT_CAP = 50.0


def _sbufify(m):
    R, C = m.shape
    n = R // 128
    return np.ascontiguousarray(m.reshape(n, 128, C).transpose(1, 0, 2).reshape(128, n * C))


def _pcol(v):
    n = v.shape[0] // 128
    return np.ascontiguousarray(v.reshape(n, 128).T)


def _xfm_x(xb):
    return np.ascontiguousarray(
        xb.reshape(NSUP, SUP, NDT, 128).transpose(0, 3, 2, 1).reshape(NSUP, 128, NDT * SUP))


def _inv_x(y):
    return np.ascontiguousarray(
        y.reshape(NSUP, 128, NDT, SUP).transpose(0, 3, 2, 1).reshape(T, D))


def _rope_tables():
    inv_freq = (1.0 / 10000.0 ** (np.arange(0, DHEAD, 2, dtype=np.float64) / DHEAD))
    t = np.arange(T, dtype=np.float64)
    freqs = np.outer(t, inv_freq)
    return (np.cos(freqs).T.astype(np.float32).copy(),
            np.sin(freqs).T.astype(np.float32).copy())


def _build_masks():
    masks = np.zeros((4, 128, SUP), dtype=np.float32)
    ki = np.arange(128)[:, None]
    qi = np.arange(128)[None, :]
    for m in range(4):
        for j in range(4):
            blk = masks[m][:, j * 128:(j + 1) * 128]
            if j < m:
                blk[:] = -1e9
            elif j == m:
                blk[:] = np.where(ki <= qi, 0.0, -1e9)
    return np.ascontiguousarray(masks.transpose(1, 0, 2).reshape(128, 4 * SUP))



class Tok:
    __slots__ = ("sem", "val")

    def __init__(self, sem, val):
        self.sem = sem
        self.val = val


_OWN = {"t": "T", "s": "S", "v": "V", "y": None, "g": None}
SEMS = ["T", "S", "V", "Y", "G", "SD"]


class Prog:
    def __init__(self):
        self.streams = {e: [] for e in "tsvyg"}
        self.counts = {s: 0 for s in SEMS}
        self.lastwait = {e: {} for e in "tsvyg"}
        self.active = {e: None for e in "tsvyg"}

    def op(self, e, fn, deps=(), sem=None, inc=0, sdeps=()):
        st = self.streams[e]
        for tok in deps:
            if tok is None:
                continue
            if _OWN[e] == tok.sem:
                continue
            lw = self.lastwait[e]
            if lw.get(tok.sem, 0) >= tok.val:
                continue
            lw[tok.sem] = tok.val
            st.append(("w", tok.sem, tok.val))
        for tok in sdeps:
            if tok is None:
                continue
            lw = self.lastwait[e]
            if lw.get(tok.sem, 0) >= tok.val:
                continue
            lw[tok.sem] = tok.val
            st.append(("w", tok.sem, tok.val))
        incs = []
        if inc:
            self.counts[sem] += inc
            incs.append((sem, inc))
        ent = ["o", fn, incs]
        st.append(ent)
        self.active[e] = ent
        return Tok(sem, self.counts[sem]) if inc else None

    def t(self, fn, deps=(), ms=False, sdeps=()):
        return self.op("t", fn, deps, "T", 1 if ms else 0, sdeps)

    def s(self, fn, deps=(), ms=False, sdeps=()):
        return self.op("s", fn, deps, "S", 1 if ms else 0, sdeps)

    def v(self, fn, deps=(), ms=False, sdeps=()):
        return self.op("v", fn, deps, "V", 1 if ms else 0, sdeps)

    def dma(self, e, fn, deps=()):
        sem = {"y": "Y", "g": "G", "s": "SD"}[e]
        return self.op(e, fn, deps, sem, 16)

    def barrier(self):
        toks = []
        for e in "tsvyg":
            ent = self.active[e]
            if ent is None:
                continue
            if not ent[2]:
                sem = _OWN[e] or ("Y" if e == "y" else "G")
                amt = 1 if _OWN[e] else 16
                self.counts[sem] += amt
                ent[2].append((sem, amt))
            sem, _ = ent[2][-1]
            toks.append((e, Tok(sem, self.counts[sem])))
        for e in "tsvyg":
            st = self.streams[e]
            for (pe, tok) in toks:
                if pe == e or _OWN[e] == tok.sem:
                    continue
                lw = self.lastwait[e]
                if lw.get(tok.sem, 0) >= tok.val:
                    continue
                lw[tok.sem] = tok.val
                st.append(("w", tok.sem, tok.val))
            self.active[e] = None



class Build:

    def __init__(self, dbg=()):
        self.dbg = set(dbg)
        self.nc = bass.Bass("TRN2", target_bir_lowering=False)
        self.p = Prog()
        self.d = {}
        self.sb = {}
        self.ctx = None

    def dram_in(self, name, shape, dtype):
        self.d[name] = self.nc.dram_tensor(name, shape, dtype, kind="ExternalInput").ap()
        return self.d[name]

    def dram_scratch(self, name, shape, dtype):
        kind = "ExternalOutput" if name in self.dbg else "Internal"
        self.d[name] = self.nc.dram_tensor(name, shape, dtype, kind=kind).ap()
        return self.d[name]

    def dram_out(self, name, shape, dtype):
        self.d[name] = self.nc.dram_tensor(name, shape, dtype, kind="ExternalOutput").ap()
        return self.d[name]


def declare_tensors(b: Build):
    b.dram_in("x", [NSUP, 128, NDT * SUP], F32)
    b.dram_out("y_out", [NSUP, 128, NDT * SUP], F32)
    b.dram_in("WuT", [128, NDT * D], F32)
    b.dram_in("WvT", [128, NDT * D], F32)
    b.dram_in("Wg1T", [128, NDT * HD], F32)
    b.dram_in("Wg2T", [128, 4 * D], F32)
    b.dram_in("w0T", [128, NDT * D], F32)
    b.dram_in("bu", [128, 16], F32)
    b.dram_in("bv", [128, 16], F32)
    b.dram_in("bg1", [128, 4], F32)
    b.dram_in("bg2", [128, 16], F32)
    b.dram_in("convw", [128, 48], F32)
    b.dram_in("convb", [128, 16], F32)
    b.dram_in("b0", [128, 16], F32)
    b.dram_in("WqT", [128, NDT * D], F32)
    b.dram_in("WkT", [128, NDT * 512], F32)
    b.dram_in("WvT1", [128, NDT * 512], F32)
    b.dram_in("WoT", [128, NDT * D], F32)
    b.dram_in("bo", [128, 16], F32)
    b.dram_in("qnw", [128, 1], F32)
    b.dram_in("knw", [128, 1], F32)
    b.dram_in("w_gate", [8, 128, NDT * 1024], I8)
    b.dram_in("w_up", [8, 128, NDT * 1024], I8)
    b.dram_in("w_down", [16, 128, 64 * 128], I8)
    b.dram_in("s_gate", [128, 64], F32)
    b.dram_in("s_up", [128, 64], F32)
    b.dram_in("s_down", [128, 16], F32)
    b.dram_in("ones_col", [128, 1], BF16)
    b.dram_in("ones_row", [1, 128], BF16)
    b.dram_in("perm128", [128, 128], BF16)
    b.dram_in("cosT", [64, T], F32)
    b.dram_in("sinT", [64, T], F32)
    b.dram_in("masks", [128, 4 * SUP], F32)
    b.dram_scratch("bwT_s", [128, NDT * D], BF16)
    b.dram_scratch("xnT0_s", [NSUP, 128, NDT * SUP], BF16)
    b.dram_scratch("gatesT_s", [NSUP, 128, NDT * SUP], BF16)
    b.dram_scratch("kcT_s", [NSUP, 128, NDT * SUP], BF16)
    b.dram_scratch("ugT_s", [NSUP, 128, NDT * SUP], BF16)
    b.dram_scratch("vgT_s", [NSUP, 128, NDT * SUP], BF16)
    b.dram_scratch("goT_s", [NSUP, 128, NDT * SUP], BF16)
    b.dram_scratch("xT1_s", [NSUP, 128, NDT * SUP], F32)
    b.dram_scratch("sT0_s", [NSUP, 128, NDT * SUP], BF16)
    b.dram_scratch("gT0_s", [NSUP, 128, NDT * SUP], BF16)
    b.dram_scratch("xnT1_s", [NSUP, 128, NDT * SUP], BF16)
    b.dram_scratch("qT_s", [NSUP, 128, NHEAD * SUP], BF16)
    b.dram_scratch("kT_s", [128, NKV * T], BF16)
    b.dram_scratch("vTok_s", [128, 32 * 512], BF16)
    b.dram_scratch("sT1_s", [NSUP, 128, NDT * SUP], BF16)
    b.dram_scratch("fuseT_s", [NSUP, 128, 64 * SUP], BF16)


def alloc_persistent(b: Build, ctx):
    nc = b.nc
    names = [
        ("bu", [128, 16], F32), ("bv", [128, 16], F32), ("bg1", [128, 4], F32),
        ("bg2", [128, 16], F32), ("convw", [128, 48], F32), ("convb", [128, 16], F32),
        ("b0", [128, 16], F32), ("bo", [128, 16], F32),
        ("qnw", [128, 1], F32), ("knw", [128, 1], F32),
        ("s_gate", [128, 64], F32), ("s_up", [128, 64], F32), ("s_down", [128, 16], F32),
        ("ones_col", [128, 1], BF16), ("ones_row", [1, 128], BF16),
        ("perm128", [128, 128], BF16),
    ]
    for (n, shape, dt) in names:
        b.sb[n] = ctx.enter_context(nc.sbuf_tensor(f"c_{n}", shape, dt))
    for (n, shape, dt) in names:
        b.p.dma("y", (lambda nn: lambda e: e.dma_start(out=b.sb[nn][:], in_=b.d[nn]))(n))
    b.sb["carry"] = ctx.enter_context(nc.sbuf_tensor("carry", [128, 16], F32))
    b.sb["zeros"] = ctx.enter_context(nc.sbuf_tensor("zeros", [128, SUP], BF16))
    b.p.v(lambda e: e.memset(b.sb["zeros"][:], 0.0))
    b.p.v(lambda e: e.memset(b.sb["carry"][:], 0.0))
    b.sb["halo"] = ctx.enter_context(nc.sbuf_tensor("halo", [128, 2 * NSUP * NDT], BF16))
    b.cc = {}
    ccvals = [EPS_LN, -0.5 * float(np.log(128.0)), -25.0]
    b.sb["ccols"] = ctx.enter_context(nc.sbuf_tensor("ccols", [128, len(ccvals)], F32))
    for i, vv in enumerate(ccvals):
        b.cc[vv] = b.sb["ccols"][:, i:i + 1]
        b.p.v((lambda ii, v2: lambda e: e.memset(b.sb["ccols"][:, ii:ii + 1], v2))(i, vv))


def mask_ap(b, m):
    return b.sb["masks"][:, m * SUP:(m + 1) * SUP]


def ln_stats_and_norm(b, ctx, p, xbf, xnb, pr, pb, tag):
    nc = b.nc
    toks_sx = None
    for j in range(NDT):
        toks_sx = p.t((lambda jj: lambda e: e.matmul(
            pr["sx"][0:1, :], b.sb["ones_col"][:], xbf[:, jj * SUP:(jj + 1) * SUP],
            start=(jj == 0), stop=(jj == NDT - 1)))(j),
            deps=[pr["sx_free"], pr["ld"]] if j == 0 else (), ms=(j == NDT - 1))
    sq_toks = []
    mm2_toks = [None, None]
    toks_sx2 = None
    for j in range(NDT):
        dep0 = [mm2_toks[j % 2]]
        if j == 0:
            dep0 += [pr["ld"], pr["x2_free"]]
        sq_toks.append(p.s((lambda jj: lambda e: e.activation(
            pb["x2"][:, (jj % 2) * SUP:(jj % 2 + 1) * SUP],
            xbf[:, jj * SUP:(jj + 1) * SUP], AF.Square))(j),
            deps=dep0, ms=True))
        toks_sx2 = p.t((lambda jj: lambda e: e.matmul(
            pr["sx2"][0:1, :], b.sb["ones_col"][:],
            pb["x2"][:, (jj % 2) * SUP:(jj % 2 + 1) * SUP],
            start=(jj == 0), stop=(jj == NDT - 1)))(j),
            deps=[sq_toks[j], pr["sx2_free"]], ms=True)
        mm2_toks[j % 2] = toks_sx2
    pr["x2_free"] = toks_sx2
    t_m = p.s(lambda e: e.activation(pb["mrow"][0:1, :], pr["sx"][0:1, :], AF.Copy,
                                     scale=1.0 / D), deps=[toks_sx], ms=True)
    t_msq = p.v(lambda e: e.tensor_tensor(pb["msq"][0:1, :], pb["mrow"][0:1, :],
                                          pb["mrow"][0:1, :], op=ALU.mult),
                deps=[t_m], ms=True)
    t_var = p.v(lambda e: e.scalar_tensor_tensor(
        pb["var"][0:1, :], pr["sx2"][0:1, :], 1.0 / D, pb["msq"][0:1, :],
        op0=ALU.mult, op1=ALU.subtract), deps=[toks_sx2, t_msq], ms=True)
    t_ln = p.s(lambda e: e.activation(pb["var"][0:1, :], pb["var"][0:1, :], AF.Ln,
                                      bias=b.cc[EPS_LN][0:1, :]), deps=[t_var], ms=True)
    t_a = p.s(lambda e: e.activation(pb["arow"][0:1, :], pb["var"][0:1, :], AF.Exp,
                                     scale=-0.5), deps=[t_ln], ms=True)
    t_b1 = p.v(lambda e: e.tensor_tensor(pb["brow_f"][0:1, :], pb["mrow"][0:1, :],
                                         pb["arow"][0:1, :], op=ALU.mult),
               deps=[t_a], ms=True)
    t_b = p.v(lambda e: e.tensor_scalar(pb["brow"][0:1, :], pb["brow_f"][0:1, :],
                                        -1.0, None, op0=ALU.mult), deps=[t_b1], ms=True)
    t_abc = p.t(lambda e: e.matmul(pr["abc"][:], b.sb["ones_row"][:], pb["arow"][0:1, :],
                                   start=True, stop=True),
                deps=[t_a, pr["abc_free"]], ms=True)
    t_bbc = p.t(lambda e: e.matmul(pr["bbc"][:], b.sb["ones_row"][:], pb["brow"][0:1, :],
                                   start=True, stop=True),
                deps=[t_b, pr["bbc_free"]], ms=True)
    t_ae = p.s(lambda e: e.activation(pb["abc_sb"][:], pr["abc"][:], AF.Copy),
               deps=[t_abc], ms=True)
    t_be = p.s(lambda e: e.activation(pb["bbc_sb"][:], pr["bbc"][:], AF.Copy),
               deps=[t_bbc], ms=True)
    pr["sx_free"] = t_m
    pr["sx2_free"] = t_var
    pr["abc_free"] = t_ae
    pr["bbc_free"] = t_be
    tok = None
    xn_free = pr.get("xn_free") or []
    for j in range(NDT):
        p.v((lambda jj: lambda e: e.tensor_tensor(
            xnb[:, jj * SUP:(jj + 1) * SUP], xbf[:, jj * SUP:(jj + 1) * SUP],
            pb["abc_sb"][:], op=ALU.mult))(j),
            deps=[t_ae] + (list(xn_free) if j == 0 else []))
        tok = p.v((lambda jj: lambda e: e.tensor_tensor(
            xnb[:, jj * SUP:(jj + 1) * SUP], xnb[:, jj * SUP:(jj + 1) * SUP],
            pb["bbc_sb"][:], op=ALU.add))(j), deps=[t_be], ms=(j == NDT - 1))
    return tok


def emit_P0(b: Build, ctx):
    nc, p = b.nc, b.p
    with (
        nc.sbuf_tensor("p0_w", [128, NDT * D], BF16) as wbuf,
        nc.sbuf_tensor("p0_abs", [128, NDT * 512], BF16) as absb,
        nc.sbuf_tensor("p0_arow", [1, D], BF16) as arow,
        nc.sbuf_tensor("p0_abc", [128, D], BF16) as abc,
        nc.sbuf_tensor("p0_t1", [128, D], F32) as t1,
        nc.psum_tensor("p0_ps", [128, 512], F32) as ps,
        nc.psum_tensor("p0_ps2", [128, 512], F32) as ps2,
    ):
        ld = p.dma("g", lambda e: e.dma_start(out=wbuf[:], in_=b.d["w0T"],
                                              max_dma_last_dim=8192))
        tar = tae = None
        for oc in range(4):
            ab_toks = []
            for j in range(NDT):
                ab_toks.append(p.s((lambda jj, occ: lambda e: e.activation(
                    absb[:, jj * 512:(jj + 1) * 512],
                    wbuf[:, jj * D + occ * 512: jj * D + (occ + 1) * 512], AF.Abs))(j, oc),
                    deps=[ld], ms=True))
            tma = None
            for j in range(NDT):
                tma = p.t((lambda jj, occ: lambda e: e.matmul(
                    ps[0:1, :], b.sb["ones_col"][:], absb[:, jj * 512:(jj + 1) * 512],
                    start=(jj == 0), stop=(jj == NDT - 1)))(j, oc),
                    deps=[ab_toks[j], tar] if j == 0 else [ab_toks[j]],
                    ms=(j == NDT - 1))
            tar = p.s((lambda occ: lambda e: e.activation(
                arow[0:1, occ * 512:(occ + 1) * 512], ps[0:1, :], AF.Copy,
                scale=1.0 / D))(oc), deps=[tma], ms=True)
            tab = p.t((lambda occ: lambda e: e.matmul(
                ps2[:], b.sb["ones_row"][:], arow[0:1, occ * 512:(occ + 1) * 512],
                start=True, stop=True))(oc), deps=[tar, tae], ms=True)
            tae = p.s((lambda occ: lambda e: e.activation(
                abc[:, occ * 512:(occ + 1) * 512], ps2[:], AF.Copy))(oc),
                deps=[tab], ms=True)
        st_tok = None
        for j in range(NDT):
            tg = p.v((lambda jj: lambda e: e.tensor_scalar(
                t1[:], wbuf[:, jj * D:(jj + 1) * D], 0.0, 2.0,
                op0=ALU.is_ge, op1=ALU.mult))(j), deps=[tae], ms=True)
            p.v((lambda jj: lambda e: e.tensor_tensor(
                t1[:], t1[:], abc[:], op=ALU.mult))(j))
            tb = p.v((lambda jj: lambda e: e.tensor_tensor(
                wbuf[:, jj * D:(jj + 1) * D], t1[:], abc[:], op=ALU.subtract))(j), ms=True)
            st_tok = tb
        p.dma("s", lambda e: e.dma_start(out=b.d["bwT_s"], in_=wbuf[:]), deps=[st_tok])
    p.barrier()


def emit_P1(b: Build, ctx):
    nc, p = b.nc, b.p
    with (
        nc.sbuf_tensor("p1_wg1", [128, NDT * HD], BF16) as wg1,
        nc.sbuf_tensor("p1_wg2", [128, 4 * D], BF16) as wg2,
        nc.sbuf_tensor("p1_xbf", [128, 2 * NDT * SUP], BF16) as xbf2,
        nc.sbuf_tensor("p1_xn", [128, NDT * SUP], BF16) as xnb,
        nc.sbuf_tensor("p1_x2", [128, 2 * SUP], BF16) as x2b,
        nc.sbuf_tensor("p1_h", [128, 4 * SUP], BF16) as hb,
        nc.sbuf_tensor("p1_g", [128, NDT * SUP], BF16) as gb,
        nc.sbuf_tensor("p1_rows", [1, 8 * SUP], F32) as rows,
        nc.psum_tensor("p1_pa", [128, 512], F32) as pa,
        nc.psum_tensor("p1_pb", [128, 512], F32) as pbk,
        nc.psum_tensor("p1_sx", [128, 512], F32) as psx,
        nc.psum_tensor("p1_sx2", [128, 512], F32) as psx2,
        nc.psum_tensor("p1_abc", [128, 512], F32) as pabc,
        nc.psum_tensor("p1_bbc", [128, 512], F32) as pbbc,
    ):
        lw1 = p.dma("g", lambda e: e.dma_start(out=wg1[:], in_=b.d["Wg1T"],
                                               max_dma_last_dim=8192))
        lw2 = p.dma("g", lambda e: e.dma_start(out=wg2[:], in_=b.d["Wg2T"],
                                               max_dma_last_dim=8192))
        pr = {"sx": psx, "sx2": psx2, "abc": pabc, "bbc": pbbc,
              "sx_free": None, "sx2_free": None, "abc_free": None, "bbc_free": None,
              "x2_free": None, "xn_free": None}
        pb = {"x2": x2b,
              "mrow": rows[:, 0 * SUP:1 * SUP], "msq": rows[:, 1 * SUP:2 * SUP],
              "var": rows[:, 2 * SUP:3 * SUP], "arow": None, "brow_f": rows[:, 4 * SUP:5 * SUP],
              "brow": None, "abc_sb": None, "bbc_sb": None}
        with (
            nc.sbuf_tensor("p1_arow", [1, SUP], BF16) as arow_t,
            nc.sbuf_tensor("p1_brow", [1, SUP], BF16) as brow_t,
            nc.sbuf_tensor("p1_absb", [128, SUP], BF16) as abc_sb,
            nc.sbuf_tensor("p1_bbsb", [128, SUP], BF16) as bbc_sb,
        ):
            pb["arow"] = arow_t
            pb["brow"] = brow_t
            pb["abc_sb"] = abc_sb
            pb["bbc_sb"] = bbc_sb
            xbf_use = [None, None]
            pp = [pa, pbk]
            pp_free = [None, None]
            g_spill = None
            for s in range(NSUP):
                xbf = xbf2[:, (s % 2) * NDT * SUP:(s % 2 + 1) * NDT * SUP]
                ldx = p.dma("g", (lambda ss, dst: lambda e: e.dma_start(
                    out=dst, in_=b.d["x"][ss], max_dma_last_dim=8192))(s, xbf),
                    deps=[xbf_use[s % 2]])
                pr["ld"] = ldx
                tok_xn = ln_stats_and_norm(b, ctx, p, xbf, xnb, pr, pb, f"p1s{s}")
                xbf_use[s % 2] = tok_xn
                st_xn = p.dma("s", (lambda ss: lambda e: e.dma_start(
                    out=b.d["xnT0_s"][ss], in_=xnb[:]))(s), deps=[tok_xn])
                p.v((lambda ss: lambda e: e.tensor_copy(
                    b.sb["halo"][:, ss * NDT:(ss + 1) * NDT],
                    xnb[:, 0:NDT * SUP:SUP]))(s))
                p.v((lambda ss: lambda e: e.tensor_copy(
                    b.sb["halo"][:, NSUP * NDT + ss * NDT:NSUP * NDT + (ss + 1) * NDT],
                    xnb[:, SUP - 1:NDT * SUP:SUP]))(s))
                h_toks = []
                tm_h_last = None
                for ht in range(4):
                    pk = pp[ht % 2]
                    tm = None
                    for j in range(NDT):
                        tm = p.t((lambda jj, hh, pkk: lambda e: e.matmul(
                            pkk[:], wg1[:, jj * HD + hh * 128: jj * HD + (hh + 1) * 128],
                            xnb[:, jj * SUP:(jj + 1) * SUP],
                            start=(jj == 0), stop=(jj == NDT - 1)))(j, ht, pk),
                            deps=([lw1, tok_xn, pp_free[ht % 2]] if j == 0 else ()),
                            ms=(j == NDT - 1))
                    te = p.s((lambda hh, pkk: lambda e: e.activation(
                        hb[:, hh * SUP:(hh + 1) * SUP], pkk[:], AF.Relu,
                        bias=b.sb["bg1"][:, hh:hh + 1]))(ht, pk), deps=[tm], ms=True)
                    pp_free[ht % 2] = te
                    h_toks.append(te)
                    tm_h_last = tm
                pr["xn_free"] = [tm_h_last, st_xn]
                g_tok = None
                for rt in range(NDT):
                    pk = pp[rt % 2]
                    tm = None
                    for ht in range(4):
                        tm = p.t((lambda hh, rr, pkk: lambda e: e.matmul(
                            pkk[:], wg2[:, hh * D + rr * 128: hh * D + (rr + 1) * 128],
                            hb[:, hh * SUP:(hh + 1) * SUP],
                            start=(hh == 0), stop=(hh == 3)))(ht, rt, pk),
                            deps=[lw2, h_toks[ht]] + ([pp_free[rt % 2]] if ht == 0 else []),
                            ms=(ht == 3))
                    te = p.s((lambda rr, pkk: lambda e: e.activation(
                        gb[:, rr * SUP:(rr + 1) * SUP], pkk[:], AF.Sigmoid,
                        bias=b.sb["bg2"][:, rr:rr + 1]))(rt, pk),
                        deps=[tm] + ([g_spill] if rt == 0 else []), ms=True)
                    pp_free[rt % 2] = te
                    g_tok = te
                g_spill = p.dma("s", (lambda ss: lambda e: e.dma_start(
                    out=b.d["gatesT_s"][ss], in_=gb[:]))(s), deps=[g_tok])
    p.barrier()


def emit_P2(b: Build, ctx):
    nc, p = b.nc, b.p
    from contextlib import ExitStack
    with ExitStack() as est:
        def al(name, shape, dt):
            return est.enter_context(nc.sbuf_tensor(name, shape, dt))
        wu = al("p2_wu", [128, NDT * D], BF16)
        wv = al("p2_wv", [128, NDT * D], BF16)
        xn = al("p2_xn", [128, NDT * SUP], BF16)
        gt = al("p2_gt", [128, NDT * SUP], BF16)
        ugb = al("p2_ug", [128, NDT * SUP], BF16)
        kcb = al("p2_kc", [128, SUP], BF16)
        gob = al("p2_go", [128, NDT * SUP], BF16)
        pa = est.enter_context(nc.psum_tensor("p2_pa", [128, 512], F32))
        pbk = est.enter_context(nc.psum_tensor("p2_pb", [128, 512], F32))
        pc = est.enter_context(nc.psum_tensor("p2_pc", [128, 512], F32))
        pd = est.enter_context(nc.psum_tensor("p2_pd", [128, 512], F32))
        lwu = p.dma("g", lambda e: e.dma_start(out=wu[:], in_=b.d["WuT"],
                                               max_dma_last_dim=8192))
        lwv = p.dma("g", lambda e: e.dma_start(out=wv[:], in_=b.d["WvT"],
                                               max_dma_last_dim=8192))
        use = []
        ppu = [pa, pbk]
        ppv = [pc, pd]
        ppu_free = [None, None]
        ppv_free = [None, None]
        carry_cp = [None] * NDT
        o_st = None
        for s in range(NSUP):
            lx = p.dma("y", (lambda ss: lambda e: e.dma_start(
                out=xn[:], in_=b.d["xnT0_s"][ss]))(s), deps=use)
            lg = p.dma("y", (lambda ss: lambda e: e.dma_start(
                out=gt[:], in_=b.d["gatesT_s"][ss]))(s), deps=use)
            last_v = None
            last_mm = None
            for rt in range(NDT):
                pku = ppu[rt % 2]
                pkv = ppv[rt % 2]
                tmu = tmv = None
                for j in range(NDT):
                    tmu = p.t((lambda jj, rr, pkk: lambda e: e.matmul(
                        pkk[:], wu[:, jj * D + rr * 128: jj * D + (rr + 1) * 128],
                        xn[:, jj * SUP:(jj + 1) * SUP],
                        start=(jj == 0), stop=(jj == NDT - 1)))(j, rt, pku),
                        deps=([lwu, lx, ppu_free[rt % 2]] if j == 0 else ()),
                        ms=(j == NDT - 1))
                for j in range(NDT):
                    tmv = p.t((lambda jj, rr, pkk: lambda e: e.matmul(
                        pkk[:], wv[:, jj * D + rr * 128: jj * D + (rr + 1) * 128],
                        xn[:, jj * SUP:(jj + 1) * SUP],
                        start=(jj == 0), stop=(jj == NDT - 1)))(j, rt, pkv),
                        deps=([lwv, lx, ppv_free[rt % 2]] if j == 0 else ()),
                        ms=(j == NDT - 1))
                last_mm = tmv
                tug = p.v((lambda rr, pkk: lambda e: e.scalar_tensor_tensor(
                    ugb[:, rr * SUP:(rr + 1) * SUP], pkk[:], b.sb["bu"][:, rr:rr + 1],
                    gt[:, rr * SUP:(rr + 1) * SUP], op0=ALU.add, op1=ALU.mult))(rt, pku),
                    deps=[tmu, lg], ms=True)
                ppu_free[rt % 2] = tug
                tvg = p.v((lambda rr, pkk: lambda e: e.scalar_tensor_tensor(
                    gob[:, rr * SUP:(rr + 1) * SUP], pkk[:], b.sb["bv"][:, rr:rr + 1],
                    gt[:, rr * SUP:(rr + 1) * SUP], op0=ALU.add, op1=ALU.mult))(rt, pkv),
                    deps=[tmv, lg] + ([o_st] if rt == 0 else []), ms=True)
                ppv_free[rt % 2] = tvg
                p.v((lambda rr: lambda e: e.tensor_tensor_scan(
                    kcb[:], ugb[:, rr * SUP:(rr + 1) * SUP],
                    b.sb["zeros"][:], 0.0, op0=ALU.add, op1=ALU.add))(rt))
                tadd = p.v((lambda rr: lambda e: e.tensor_scalar(
                    kcb[:], kcb[:],
                    b.sb["carry"][:, rr:rr + 1], None, op0=ALU.add))(rt),
                    sdeps=[carry_cp[rt]], ms=True)
                carry_cp[rt] = p.v((lambda rr: lambda e: e.tensor_copy(
                    b.sb["carry"][:, rr:rr + 1],
                    kcb[:, SUP - 1:SUP]))(rt),
                    sdeps=[tadd], ms=True)
                last_v = p.v((lambda rr: lambda e: e.tensor_tensor(
                    gob[:, rr * SUP:(rr + 1) * SUP], gob[:, rr * SUP:(rr + 1) * SUP],
                    kcb[:], op=ALU.mult))(rt), ms=True)
            o_st = p.dma("s", (lambda ss: lambda e: e.dma_start(
                out=b.d["goT_s"][ss], in_=gob[:]))(s), deps=[last_v])
            use = [last_mm, last_v, o_st]
    p.barrier()


def emit_P3(b: Build, ctx):
    nc, p = b.nc, b.p
    with (
        nc.sbuf_tensor("p3_wun", [128, NDT * D], BF16) as wun,
        nc.sbuf_tensor("p3_bw", [128, NDT * D], BF16) as bw,
        nc.sbuf_tensor("p3_xn", [128, NDT * SUP], BF16) as xn2,
        nc.sbuf_tensor("p3_go", [128, NDT * SUP], BF16) as go2,
        nc.sbuf_tensor("p3_x", [128, NDT * SUP], F32) as xf2,
        nc.sbuf_tensor("p3_cv", [128, SUP], F32) as cvb,
        nc.sbuf_tensor("p3_st", [128, SUP], BF16) as stb,
        nc.sbuf_tensor("p3_gt", [128, SUP], BF16) as gtb,
        nc.psum_tensor("p3_pa", [128, 512], F32) as pa,
        nc.psum_tensor("p3_pb", [128, 512], F32) as pbk,
        nc.psum_tensor("p3_pc", [128, 512], F32) as pc,
        nc.psum_tensor("p3_pd", [128, 512], F32) as pd,
    ):
        lwu = p.dma("g", lambda e: e.dma_start(out=wun[:], in_=b.d["WuT"],
                                               max_dma_last_dim=8192))
        lbw = p.dma("y", lambda e: e.dma_start(out=bw[:], in_=b.d["bwT_s"]))
        use = [[]]
        ppa = [pa, pbk]
        ppb = [pc, pd]
        ppa_free = [None, None]
        ppb_free = [None, None]
        gtb_free = None
        for s in range(NSUP):
            W = NDT * SUP
            xn = xn2[:, 0:W]
            go = go2[:, 0:W]
            xf = xf2[:, 0:W]
            lx = p.dma("y", (lambda ss, dst: lambda e: e.dma_start(
                out=dst, in_=b.d["xnT0_s"][ss]))(s, xn), deps=use[0])
            lgo = p.dma("y", (lambda ss, dst: lambda e: e.dma_start(
                out=dst, in_=b.d["goT_s"][ss]))(s, go), deps=use[0])
            lxf = p.dma("y", (lambda ss, dst: lambda e: e.dma_start(
                out=dst, in_=b.d["x"][ss]))(s, xf), deps=use[0])
            lhl = lhr = None
            def halo_left(dd, ss=s):
                if ss == 0:
                    return b.sb["zeros"][:, 0:1]
                return b.sb["halo"][:, NSUP * NDT + (ss - 1) * NDT + dd:
                                    NSUP * NDT + (ss - 1) * NDT + dd + 1]
            def halo_right(dd, ss=s):
                if ss == NSUP - 1:
                    return b.sb["zeros"][:, 0:1]
                return b.sb["halo"][:, (ss + 1) * NDT + dd:(ss + 1) * NDT + dd + 1]
            x_done = None
            for dt in range(NDT):
                pku = ppa[dt % 2]
                pkf = ppb[dt % 2]
                tmu = tmf = None
                for rt in range(NDT):
                    tmu = p.t((lambda rr, dd, pkk, goo: lambda e: e.matmul(
                        pkk[:], wun[:, rr * D + dd * 128: rr * D + (dd + 1) * 128],
                        goo[:, rr * SUP:(rr + 1) * SUP],
                        start=(rr == 0), stop=(rr == NDT - 1)))(rt, dt, pku, go),
                        deps=([lwu, lgo, ppa_free[dt % 2]] if rt == 0 else ()),
                        ms=(rt == NDT - 1))
                for it in range(NDT):
                    tmf = p.t((lambda ii, dd, pkk, xnn: lambda e: e.matmul(
                        pkk[:], bw[:, ii * D + dd * 128: ii * D + (dd + 1) * 128],
                        xnn[:, ii * SUP:(ii + 1) * SUP],
                        start=(ii == 0), stop=(ii == NDT - 1)))(it, dt, pkf, xn),
                        deps=([lbw, lx, ppb_free[dt % 2]] if it == 0 else ()),
                        ms=(it == NDT - 1))
                xs = xn[:, dt * SUP:(dt + 1) * SUP]
                tc0 = p.v((lambda dd, xss: lambda e: e.tensor_scalar(
                    cvb[:, 1:SUP], xss[:, 0:SUP - 1], b.sb["convw"][:, dd * 3:dd * 3 + 1],
                    None, op0=ALU.mult))(dt, xs), deps=[lx], ms=True)
                p.v((lambda dd, apl: lambda e: e.tensor_scalar(
                    cvb[:, 0:1], apl,
                    b.sb["convw"][:, dd * 3:dd * 3 + 1], None,
                    op0=ALU.mult))(dt, halo_left(dt)))
                p.v((lambda dd, xss: lambda e: e.scalar_tensor_tensor(
                    cvb[:], xss[:], b.sb["convw"][:, dd * 3 + 1:dd * 3 + 2], cvb[:],
                    op0=ALU.mult, op1=ALU.add))(dt, xs))
                p.v((lambda dd, xss: lambda e: e.scalar_tensor_tensor(
                    cvb[:, 0:SUP - 1], xss[:, 1:SUP], b.sb["convw"][:, dd * 3 + 2:dd * 3 + 3],
                    cvb[:, 0:SUP - 1], op0=ALU.mult, op1=ALU.add))(dt, xs))
                tcv = p.v((lambda dd, apr: lambda e: e.scalar_tensor_tensor(
                    cvb[:, SUP - 1:SUP], apr,
                    b.sb["convw"][:, dd * 3 + 2:dd * 3 + 3], cvb[:, SUP - 1:SUP],
                    op0=ALU.mult, op1=ALU.add))(dt, halo_right(dt)), ms=True)
                ts = p.v((lambda dd, pkk: lambda e: e.scalar_tensor_tensor(
                    stb[:], cvb[:], b.sb["convb"][:, dd:dd + 1], pkk[:],
                    op0=ALU.add, op1=ALU.add))(dt, pku), deps=[tmu, tcv], ms=True)
                ppa_free[dt % 2] = ts
                tg = p.s((lambda dd, pkk: lambda e: e.activation(
                    gtb[:], pkk[:], AF.Silu, bias=b.sb["b0"][:, dd:dd + 1]))(dt, pkf),
                    deps=[tmf, gtb_free], ms=True)
                ppb_free[dt % 2] = tg
                dbg_deps = []
                if "sT0_s" in b.dbg:
                    dbg_deps.append(p.dma("s", (lambda ss, dd: lambda e: e.dma_start(
                        out=b.d["sT0_s"][ss][:, dd * SUP:(dd + 1) * SUP], in_=stb[:]))(s, dt),
                        deps=[ts]))
                    dbg_deps.append(p.dma("s", (lambda ss, dd: lambda e: e.dma_start(
                        out=b.d["gT0_s"][ss][:, dd * SUP:(dd + 1) * SUP], in_=gtb[:]))(s, dt),
                        deps=[tg]))
                tprod = p.v((lambda dd: lambda e: e.tensor_tensor(
                    stb[:], stb[:], gtb[:], op=ALU.mult))(dt), deps=[tg, ts] + dbg_deps, ms=True)
                gtb_free = tprod
                x_done = p.v((lambda dd, xff: lambda e: e.tensor_tensor(
                    xff[:, dd * SUP:(dd + 1) * SUP], xff[:, dd * SUP:(dd + 1) * SUP],
                    stb[:], op=ALU.add))(dt, xf), deps=[tprod, lxf], ms=True)
            st_x = p.dma("s", (lambda ss, xff: lambda e: e.dma_start(
                out=b.d["xT1_s"][ss], in_=xff))(s, xf), deps=[x_done])
            use[0] = [x_done, st_x]
    p.barrier()




def emit_P4(b: Build, ctx):
    nc, p = b.nc, b.p
    from contextlib import ExitStack
    with ExitStack() as est:
        def al(name, shape, dt):
            return est.enter_context(nc.sbuf_tensor(name, shape, dt))
        def alp(name):
            return est.enter_context(nc.psum_tensor(name, [128, 512], F32))
        wq = al("p4_wq", [128, NDT * D], BF16)
        wk = al("p4_wk", [128, NDT * 512], BF16)
        wv = al("p4_wv", [128, NDT * 512], BF16)
        c2 = al("p4_c2", [128, T], BF16)
        s2 = al("p4_s2", [128, T], BF16)
        xbf = al("p4_xbf", [128, NDT * SUP], BF16)
        xnb = al("p4_xn", [128, NDT * SUP], BF16)
        x2b = al("p4_x2", [128, 2 * SUP], BF16)
        qbuf = al("p4_q", [128, NHEAD * SUP], BF16)
        kbuf = al("p4_k", [128, NKV * SUP], BF16)
        vbuf = al("p4_v", [128, NKV * SUP], BF16)
        rows = al("p4_rows", [1, 6 * SUP], F32)
        arow_t = al("p4_arow", [1, SUP], BF16)
        brow_t = al("p4_brow", [1, SUP], BF16)
        abc_sb = al("p4_absb", [128, SUP], BF16)
        bbc_sb = al("p4_bbsb", [128, SUP], BF16)
        invrow = al("p4_inv", [1, 2 * SUP], BF16)
        qsb = al("p4_qsb", [128, 2 * SUP], BF16)
        qnb = al("p4_qn", [128, SUP], BF16)
        zswb = al("p4_zsw", [128, SUP], BF16)
        m1b = al("p4_m1", [128, SUP], BF16)
        psx = alp("p4_sx")
        psx2 = alp("p4_sx2")
        pabc = alp("p4_abc")
        pbbc = alp("p4_bbc")
        pa = alp("p4_pa")
        pbk = alp("p4_pb")
        prow = alp("p4_row")
        pbc = alp("p4_bc")
        lwq = p.dma("g", lambda e: e.dma_start(out=wq[:], in_=b.d["WqT"],
                                               max_dma_last_dim=8192))
        lwk = p.dma("g", lambda e: e.dma_start(out=wk[:], in_=b.d["WkT"],
                                               max_dma_last_dim=8192))
        lwv = p.dma("g", lambda e: e.dma_start(out=wv[:], in_=b.d["WvT1"],
                                               max_dma_last_dim=8192))
        lc1 = p.dma("g", lambda e: e.dma_start(out=c2[0:64, :], in_=b.d["cosT"],
                                               max_dma_last_dim=8192))
        lc2 = p.dma("g", lambda e: e.dma_start(out=c2[64:128, :], in_=b.d["cosT"],
                                               max_dma_last_dim=8192))
        ls1 = p.dma("g", lambda e: e.dma_start(out=s2[0:64, :], in_=b.d["sinT"],
                                               max_dma_last_dim=8192))
        ls2 = p.dma("g", lambda e: e.dma_start(out=s2[64:128, :], in_=b.d["sinT"],
                                               max_dma_last_dim=8192))
        tneg = p.v(lambda e: e.tensor_scalar(s2[64:128, :], s2[64:128, :], -1.0, None,
                                             op0=ALU.mult), deps=[ls2], ms=True)
        pr = {"sx": psx, "sx2": psx2, "abc": pabc, "bbc": pbbc,
              "sx_free": None, "sx2_free": None, "abc_free": None, "bbc_free": None,
              "x2_free": None, "xn_free": None}
        pb = {"x2": x2b,
              "mrow": rows[:, 0 * SUP:1 * SUP], "msq": rows[:, 1 * SUP:2 * SUP],
              "var": rows[:, 2 * SUP:3 * SUP], "arow": arow_t,
              "brow_f": rows[:, 4 * SUP:5 * SUP],
              "brow": brow_t, "abc_sb": abc_sb, "bbc_sb": bbc_sb}
        rmsrow = rows[:, 5 * SUP:6 * SUP]
        xbf_use = [[]]
        inv_free = None
        inv_bc_free = [None]
        row_free = None
        swap_free = None
        qsb_free = None
        qn_free = None
        zsw_free = None
        st_q = st_k = st_v = st_xn = None
        proj_free = [None, None]
        for s in range(NSUP):
            ldx = p.dma("g", (lambda ss: lambda e: e.dma_start(
                out=xbf[:], in_=b.d["xT1_s"][ss], max_dma_last_dim=8192))(s),
                deps=xbf_use[0])
            pr["ld"] = ldx
            tok_xn = ln_stats_and_norm(b, ctx, p, xbf[:], xnb, pr, pb, f"p4s{s}")
            st_xn = p.dma("s", (lambda ss: lambda e: e.dma_start(
                out=b.d["xnT1_s"][ss], in_=xnb[:]))(s), deps=[tok_xn])
            pp = [pa, pbk]
            pp_free = proj_free
            last_store = None
            qk_plan = [("q", ht, wq, D, b.sb["qnw"], qbuf) for ht in range(NHEAD)] + \
                      [("k", g, wk, 512, b.sb["knw"], kbuf) for g in range(NKV)]

            def emit_proj(idx):
                nonlocal row_free
                kind, ht, wmat, wcols, nw, obuf = qk_plan[idx]
                pk = pp[idx % 2]
                tm = None
                for j in range(NDT):
                    tm = p.t((lambda jj, hh, pkk, wm, wc: lambda e: e.matmul(
                        pkk[:], wm[:, jj * wc + hh * 128: jj * wc + (hh + 1) * 128],
                        xnb[:, jj * SUP:(jj + 1) * SUP],
                        start=(jj == 0), stop=(jj == NDT - 1)))(j, ht, pk, wmat, wcols),
                        deps=([lwq, lwk, tok_xn, pp_free[idx % 2]] if j == 0 else ()),
                        ms=(j == NDT - 1))
                tsq = p.s((lambda pkk: lambda e: e.activation(
                    x2b[:, 0:SUP], pkk[:], AF.Square))(pk),
                    deps=[tm, pr["x2_free"], qsb_free], ms=True)
                tev = p.s((lambda pkk, qs2: lambda e: e.activation(
                    qs2, pkk[:], AF.Copy))(pk, qsb[:, (idx % 2) * SUP:(idx % 2 + 1) * SUP]),
                    ms=True)
                pp_free[idx % 2] = tev
                tssq = p.t(lambda e: e.matmul(prow[0:1, :], b.sb["ones_col"][:],
                                              x2b[:, 0:SUP], start=True, stop=True),
                           deps=[tsq, row_free], ms=True)
                pr["x2_free"] = tssq
                t1r = p.s((lambda rr2: lambda e: e.activation(
                    rr2[0:1, :], prow[0:1, :], AF.Ln))(rmsrow), deps=[tssq], ms=True)
                row_free = t1r
                p.s((lambda rr2: lambda e: e.activation(
                    rr2[0:1, :], rr2[0:1, :], AF.Exp, scale=0.5,
                    bias=b.cc[-0.5 * float(np.log(128.0))][0:1, :]))(rmsrow), ms=True)
                t3r = p.v((lambda rr2: lambda e: e.tensor_scalar(
                    rr2[0:1, :], rr2[0:1, :], EPS_RMS, None, op0=ALU.add))(rmsrow),
                    ms=True)
                p.s((lambda rr2: lambda e: e.activation(
                    rr2[0:1, :], rr2[0:1, :], AF.Ln))(rmsrow), deps=[t3r], ms=True)
                t5r = p.s((lambda iv: lambda e: e.activation(
                    iv, rmsrow[0:1, :], AF.Exp, scale=-1.0))(
                        invrow[0:1, (idx % 2) * SUP:(idx % 2 + 1) * SUP]),
                    deps=[inv_free], ms=True)
                return (idx, tev, t1r, t5r)

            def emit_rope(stg):
                nonlocal inv_free, swap_free, qn_free, zsw_free, qsb_free
                idx, tev, t1r, t5r = stg
                kind, ht, wmat, wcols, nw, obuf = qk_plan[idx]
                tbc = p.t((lambda iv: lambda e: e.matmul(
                    pbc[:], b.sb["ones_row"][:], iv, start=True, stop=True))(
                        invrow[0:1, (idx % 2) * SUP:(idx % 2 + 1) * SUP]),
                    deps=[t5r, inv_bc_free[0]], ms=True)
                inv_free = tbc
                tqn = p.v((lambda nww, qs2: lambda e: e.scalar_tensor_tensor(
                    qnb[:], qs2, nww[:, 0:1], pbc[:], op0=ALU.mult, op1=ALU.mult))(
                        nw, qsb[:, (idx % 2) * SUP:(idx % 2 + 1) * SUP]),
                    deps=[tev, tbc, qn_free], ms=True)
                inv_bc_free[0] = tqn
                tsw = p.t(lambda e: e.matmul(psx[:], b.sb["perm128"][:], qnb[:],
                                             start=True, stop=True),
                          deps=[tqn, swap_free, pr["sx_free"]], ms=True)
                tze = p.s(lambda e: e.activation(zswb[:], psx[:], AF.Copy),
                          deps=[tsw, zsw_free], ms=True)
                swap_free = tze
                pr["sx_free"] = tze
                tm1 = p.v((lambda ss2: lambda e: e.tensor_tensor(
                    m1b[:], qnb[:], c2[:, ss2 * SUP:(ss2 + 1) * SUP], op=ALU.mult))(s),
                    deps=[lc1, lc2], ms=True)
                qn_free = tsw
                tm2 = p.v((lambda ss2, hh, ob: lambda e: e.tensor_tensor(
                    ob[:, hh * SUP:(hh + 1) * SUP], zswb[:],
                    s2[:, ss2 * SUP:(ss2 + 1) * SUP], op=ALU.mult))(s, ht, obuf),
                    deps=[tze, tneg, ls1] + ([st_q, st_k] if idx == 0 else []), ms=True)
                zsw_free = tm2
                tro = p.v((lambda hh, ob: lambda e: e.tensor_tensor(
                    ob[:, hh * SUP:(hh + 1) * SUP], ob[:, hh * SUP:(hh + 1) * SUP],
                    m1b[:], op=ALU.add))(ht, obuf), ms=True)
                qsb_free = tro
                return tro

            pend_h = None
            last_store = None
            for idx in range(len(qk_plan) + 1):
                cur = emit_proj(idx) if idx < len(qk_plan) else None
                if pend_h is not None:
                    last_store = emit_rope(pend_h)
                pend_h = cur
            vev = None
            for sub in range(4):
                pk = pp[sub % 2]
                tm = None
                for j in range(NDT):
                    tm = p.t((lambda jj, sb2, pkk: lambda e: e.matmul(
                        pkk[:], xnb[:, jj * SUP + sb2 * 128: jj * SUP + (sb2 + 1) * 128],
                        wv[:, jj * 512:(jj + 1) * 512],
                        start=(jj == 0), stop=(jj == NDT - 1)))(j, sub, pk),
                        deps=([lwv, tok_xn, pp_free[sub % 2]] if j == 0 else ()),
                        ms=(j == NDT - 1))
                vev = p.s((lambda sb2, pkk: lambda e: e.activation(
                    vbuf[:, sb2 * SUP:(sb2 + 1) * SUP], pkk[:], AF.Copy))(sub, pk),
                    deps=[tm, st_v], ms=True)
                pp_free[sub % 2] = vev
            st_q = p.dma("s", (lambda ss: lambda e: e.dma_start(
                out=b.d["qT_s"][ss], in_=qbuf[:]))(s), deps=[last_store])
            st_k = p.dma("s", (lambda ss: lambda e: e.dma_start(
                out=b.d["kT_s"].rearrange("p (g t) -> p g t", g=NKV)[:, :, ss * SUP:(ss + 1) * SUP],
                in_=kbuf[:].rearrange("p (g t) -> p g t", g=NKV)))(s), deps=[last_store])
            st_v = p.dma("s", (lambda ss: lambda e: e.dma_start(
                out=b.d["vTok_s"][:, ss * 4 * SUP:(ss + 1) * 4 * SUP], in_=vbuf[:]))(s),
                deps=[vev])
            xbf_use[0] = [tok_xn]
            pr["xn_free"] = [tm, st_xn]
    p.barrier()




def emit_P5(b: Build, ctx):
    nc, p = b.nc, b.p
    from contextlib import ExitStack
    import math
    sc_tanh = 1.0 / (SOFT_CAP * math.sqrt(DHEAD))
    with ExitStack() as est:
        def al(name, shape, dt):
            return est.enter_context(nc.sbuf_tensor(name, shape, dt))
        wo = al("p5_wo", [128, NDT * D], BF16)
        kts = al("p5_kt", [128, NKV * T], BF16)
        vts = al("p5_vt", [128, 32 * 512], BF16)
        msk = al("p5_msk", [128, 4 * SUP], F32)
        qts = al("p5_q", [128, NHEAD * SUP], BF16)
        ybuf = al("p5_y", [128, NHEAD * SUP], BF16)
        sbo = al("p5_sbo", [128, NDT * SUP], BF16)
        t1b = al("p5_t1", [128, 2 * 2 * SUP], F32)
        expb = al("p5_exp", [128, 2 * 2 * SUP], BF16)
        denr = al("p5_den", [1, SUP], BF16)
        dbc = al("p5_dbc", [128, SUP], F32)
        pS = [est.enter_context(nc.psum_tensor(f"p5_s{i}", [128, 1024], F32))
              for i in range(2)]
        pden = est.enter_context(nc.psum_tensor("p5_pden", [128, 512], F32))
        pOs = [est.enter_context(nc.psum_tensor(f"p5_pO{i}", [128, 512], F32))
               for i in range(2)]
        pX = est.enter_context(nc.psum_tensor("p5_pX", [128, 512], F32))

        lwo = p.dma("g", lambda e: e.dma_start(out=wo[:], in_=b.d["WoT"],
                                               max_dma_last_dim=8192))
        lkt = p.dma("y", lambda e: e.dma_start(out=kts[:], in_=b.d["kT_s"]))
        lvt = p.dma("y", lambda e: e.dma_start(out=vts[:], in_=b.d["vTok_s"]))
        lmk = p.dma("y", lambda e: e.dma_start(out=msk[:], in_=b.d["masks"]))
        lq_tok = None
        pS_free = [None, None]
        epar_free = [None, None]
        pden_free = None
        pO_free = [None, None]
        pX_free = None
        dbc_free = None
        y_free = None
        sbo_free = None
        st_s = None
        proj_free = [None, None]
        gctr = 0

        def emit_S(h, g0, s, lq):
            nonlocal gctr
            g = h // NKV
            par = gctr % 2
            gctr += 1
            psb = pS[par]
            tpar = t1b[:, par * 2 * SUP:(par + 1) * 2 * SUP]
            epar = expb[:, par * 2 * SUP:(par + 1) * 2 * SUP]
            last_smm = None
            for ii, kb in enumerate((g0, g0 + 1)):
                last_smm = p.t((lambda kkb, iii, psbb, hh, gg: lambda e: e.matmul(
                    psbb[:, iii * 512:(iii + 1) * 512],
                    kts[:, gg * T + kkb * 128: gg * T + (kkb + 1) * 128],
                    qts[:, hh * SUP:(hh + 1) * SUP],
                    start=True, stop=True))(kb, ii, psb, h, g),
                    deps=([lq, lkt, pS_free[par]] if ii == 0 else ()), ms=(ii == 1))
            tt = p.s((lambda psbb, tp2: lambda e: e.activation(
                tp2[:], psbb[:], AF.Tanh, scale=sc_tanh))(psb, tpar),
                deps=[last_smm], ms=True)
            pS_free[par] = tt
            tmsk = None
            for ii, kb in enumerate((g0, g0 + 1)):
                if kb >= 4 * s:
                    m = kb - 4 * s
                    tmsk = p.v((lambda iii, mm, tp2: lambda e: e.tensor_tensor(
                        tp2[:, iii * SUP:(iii + 1) * SUP],
                        tp2[:, iii * SUP:(iii + 1) * SUP],
                        msk[:, mm * SUP:(mm + 1) * SUP], op=ALU.add))(ii, m, tpar),
                        deps=[tt, lmk], ms=True)
            tex = p.s((lambda ep, tp2: lambda e: e.activation(
                ep[:], tp2[:], AF.Exp, scale=SOFT_CAP,
                bias=b.cc[-25.0]))(epar, tpar),
                deps=[tt, tmsk, epar_free[par]], ms=True)
            return (h, g0, par, epar, tex)

        def emit_denav(st, s):
            nonlocal pden_free, epar_free
            h, g0, par, epar, tex = st
            g = h // NKV
            pO = pOs[h % 2]
            nkb = 4 * s + 4
            last_av = None
            for ii, kb in enumerate((g0, g0 + 1)):
                p.t((lambda iii, ep, kkb: lambda e: e.matmul(
                    pden[0:1, :], b.sb["ones_col"][:],
                    ep[:, iii * SUP:(iii + 1) * SUP],
                    start=(kkb == 0), stop=(kkb == nkb - 1)))(ii, epar, kb),
                    deps=([tex, pden_free] if kb == 0 else [tex]))
                last_av = p.t((lambda iii, ep, kkb, gg, pOO: lambda e: e.matmul(
                    pOO[:], vts[:, kkb * 512 + gg * 128: kkb * 512 + (gg + 1) * 128],
                    ep[:, iii * SUP:(iii + 1) * SUP],
                    start=(kkb == 0), stop=(kkb == nkb - 1)))(ii, epar, kb, g, pO),
                    deps=([lvt, pO_free[h % 2]] if kb == 0 else ()), ms=True)
            epar_free[par] = last_av
            return last_av

        def emit_finale(h, last_av):
            nonlocal pden_free, pX_free, dbc_free, y_free
            td = p.s(lambda e: e.activation(denr[0:1, :], pden[0:1, :], AF.Copy),
                     deps=[last_av], ms=True)
            pden_free = td
            tb2 = p.t(lambda e: e.matmul(pX[:], b.sb["ones_row"][:], denr[0:1, :],
                                         start=True, stop=True),
                      deps=[td, pX_free], ms=True)
            tr = p.v(lambda e: e.reciprocal(dbc[:], pX[:]),
                     deps=[tb2, dbc_free], ms=True)
            pX_free = tr
            ty = p.v((lambda hh, pOO: lambda e: e.tensor_tensor(
                ybuf[:, hh * SUP:(hh + 1) * SUP], pOO[:], dbc[:], op=ALU.mult))(h, pOs[h % 2]),
                deps=[last_av, tr, y_free], ms=True)
            pO_free[h % 2] = ty
            dbc_free = ty
            return ty

        for s in range(NSUP):
            lq = p.dma("y", (lambda ss: lambda e: e.dma_start(
                out=qts[:], in_=b.d["qT_s"][ss]))(s), deps=lq_tok or [])
            nkb = 4 * s + 4
            stages = [(h, g0) for h in range(NHEAD) for g0 in range(0, nkb, 2)]
            pend = None
            y_last = None
            for stg in stages + [None]:
                cur = emit_S(stg[0], stg[1], s, lq) if stg is not None else None
                if pend is not None:
                    la = emit_denav(pend, s)
                    if pend[1] + 2 >= nkb:
                        y_last = emit_finale(pend[0], la)
                pend = cur
            lq_tok = [pS_free[0], pS_free[1]]
            pp = [pX, pden]
            st_deps = None
            tm = None
            for ot in range(NDT):
                pk = pp[ot % 2]
                pfree = [pX_free, pden_free][ot % 2] if ot < 2 else proj_free[ot % 2]
                for j in range(NDT):
                    tm = p.t((lambda jj, oo, pkk: lambda e: e.matmul(
                        pkk[:], wo[:, jj * D + oo * 128: jj * D + (oo + 1) * 128],
                        ybuf[:, jj * SUP:(jj + 1) * SUP],
                        start=(jj == 0), stop=(jj == NDT - 1)))(j, ot, pk),
                        deps=([lwo, y_last, pfree] if j == 0 else ()),
                        ms=(j == NDT - 1))
                tev = p.s((lambda oo, pkk: lambda e: e.activation(
                    sbo[:, oo * SUP:(oo + 1) * SUP], pkk[:], AF.Identity,
                    bias=b.sb["bo"][:, oo:oo + 1]))(ot, pk),
                    deps=[tm] + ([sbo_free] if ot == 0 else []), ms=True)
                proj_free[ot % 2] = tev
                st_deps = tev
            pX_free = proj_free[0]
            pden_free = proj_free[1]
            y_free = tm
            st_s = p.dma("s", (lambda ss: lambda e: e.dma_start(
                out=b.d["sT1_s"][ss], in_=sbo[:]))(s), deps=[st_deps])
            sbo_free = st_s
    p.barrier()


def emit_P6(b: Build, ctx):
    nc, p = b.nc, b.p
    from contextlib import ExitStack
    with ExitStack() as est:
        def al(name, shape, dt):
            return est.enter_context(nc.sbuf_tensor(name, shape, dt))
        wg2b = al("p6_wg", [128, 2 * NDT * 1024], BF16)
        wu2b = al("p6_wu", [128, 2 * NDT * 1024], BF16)
        xnb2 = al("p6_xn", [128, 2 * NDT * SUP], BF16)
        gbuf = al("p6_g", [128, SUP], BF16)
        ubuf = al("p6_u", [128, SUP], BF16)
        fbuf = al("p6_f", [128, 8 * SUP], BF16)
        pg = [est.enter_context(nc.psum_tensor(f"p6_pg{i}", [128, 512], F32)) for i in range(2)]
        pu = [est.enter_context(nc.psum_tensor(f"p6_pu{i}", [128, 512], F32)) for i in range(2)]
        W1 = NDT * 1024
        wfree = [None, None]
        lw_toks = [None, None]
        xn_use = [None, None]
        pg_free = [None, None]
        pu_free = [None, None]
        gb_free = None
        ub_free = None
        fb_free = None
        st_f = None
        for ib in range(8):
            par = ib % 2
            wg = wg2b[:, par * W1:(par + 1) * W1]
            wu = wu2b[:, par * W1:(par + 1) * W1]
            lg2 = p.dma("g", (lambda ibb, dst: lambda e: e.dma_start(
                out=dst, in_=b.d["w_gate"][ibb], max_dma_last_dim=2048))(ib, wg),
                deps=[wfree[par]])
            lu2 = p.dma("g", (lambda ibb, dst: lambda e: e.dma_start(
                out=dst, in_=b.d["w_up"][ibb], max_dma_last_dim=2048))(ib, wu),
                deps=[wfree[par]])
            lw_toks[par] = [lg2, lu2]
            last_mm = None
            for s in range(NSUP):
                xnb = xnb2[:, (s % 2) * NDT * SUP:(s % 2 + 1) * NDT * SUP]
                lxn = p.dma("y", (lambda ss, dst: lambda e: e.dma_start(
                    out=dst, in_=b.d["xnT1_s"][ss]))(s, xnb), deps=[xn_use[s % 2]])
                tf_last = None
                for it in range(8):
                    pkg = pg[it % 2]
                    pku = pu[it % 2]
                    tmg = tmu = None
                    for j in range(NDT):
                        tmg = p.t((lambda jj, itt, pkk, wgg, xx: lambda e: e.matmul(
                            pkk[:], wgg[:, jj * 1024 + itt * 128: jj * 1024 + (itt + 1) * 128],
                            xx[:, jj * SUP:(jj + 1) * SUP],
                            start=(jj == 0), stop=(jj == NDT - 1)))(j, it, pkg, wg, xnb),
                            deps=([lg2, lxn, pg_free[it % 2]] if j == 0 else ()),
                            ms=(j == NDT - 1))
                    for j in range(NDT):
                        tmu = p.t((lambda jj, itt, pkk, wuu, xx: lambda e: e.matmul(
                            pkk[:], wuu[:, jj * 1024 + itt * 128: jj * 1024 + (itt + 1) * 128],
                            xx[:, jj * SUP:(jj + 1) * SUP],
                            start=(jj == 0), stop=(jj == NDT - 1)))(j, it, pku, wu, xnb),
                            deps=([lu2, lxn, pu_free[it % 2]] if j == 0 else ()),
                            ms=(j == NDT - 1))
                    last_mm = tmu
                    ic = ib * 8 + it
                    tg2 = p.s((lambda icc, pkk: lambda e: e.activation(
                        gbuf[:], pkk[:], AF.Silu,
                        scale=b.sb["s_gate"][:, icc:icc + 1]))(ic, pkg),
                        deps=[tmg, gb_free], ms=True)
                    pg_free[it % 2] = tg2
                    tu2 = p.s((lambda icc, pkk: lambda e: e.activation(
                        ubuf[:], pkk[:], AF.Identity,
                        scale=b.sb["s_up"][:, icc:icc + 1]))(ic, pku),
                        deps=[tmu, ub_free], ms=True)
                    pu_free[it % 2] = tu2
                    tf_last = p.v((lambda itt: lambda e: e.tensor_tensor(
                        fbuf[:, itt * SUP:(itt + 1) * SUP], gbuf[:], ubuf[:],
                        op=ALU.mult))(it),
                        deps=[tg2, tu2] + ([st_f] if it == 0 else []), ms=True)
                    gb_free = tf_last
                    ub_free = tf_last
                xn_use[s % 2] = last_mm
                st_f = p.dma("s", (lambda ss, ibb: lambda e: e.dma_start(
                    out=b.d["fuseT_s"][ss][:, ibb * 8 * SUP:(ibb + 1) * 8 * SUP],
                    in_=fbuf[:]))(s, ib), deps=[tf_last])
            wfree[par] = last_mm
    p.barrier()


def emit_P7(b: Build, ctx):
    nc, p = b.nc, b.p
    from contextlib import ExitStack
    with ExitStack() as est:
        def al(name, shape, dt):
            return est.enter_context(nc.sbuf_tensor(name, shape, dt))
        fb2 = al("p7_f", [128, 64 * SUP], BF16)
        wd2 = al("p7_wd", [128, 2 * 64 * 128], BF16)
        s1b = al("p7_s1", [128, NDT * SUP], BF16)
        xfb = al("p7_x", [128, NDT * SUP], F32)
        gti = al("p7_g", [128, SUP], BF16)
        pti = al("p7_p", [128, SUP], BF16)
        pp = [est.enter_context(nc.psum_tensor(f"p7_p{i}", [128, 512], F32)) for i in range(2)]
        WD = 64 * 128
        use = []
        wd_free = [None, None]
        pp_free = [None, None]
        g_free = None
        p_free = None
        st_y = None
        fb_use = [None, None]
        HF = 32 * SUP
        for s in range(NSUP):
            fb = fb2[:, :]
            lfA = p.dma("y", (lambda ss, dst: lambda e: e.dma_start(
                out=dst, in_=b.d["fuseT_s"][ss][:, 0:HF]))(s, fb2[:, 0:HF]),
                deps=[fb_use[0]])
            lfB = p.dma("y", (lambda ss, dst: lambda e: e.dma_start(
                out=dst, in_=b.d["fuseT_s"][ss][:, HF:2 * HF]))(s, fb2[:, HF:2 * HF]),
                deps=[fb_use[0]])
            ls1 = p.dma("y", (lambda ss: lambda e: e.dma_start(
                out=s1b[:], in_=b.d["sT1_s"][ss]))(s), deps=use)
            lxf = p.dma("y", (lambda ss: lambda e: e.dma_start(
                out=xfb[:], in_=b.d["xT1_s"][ss]))(s), deps=use)
            x_done = None
            last_mm = None
            for ht in range(NDT):
                idx = s * NDT + ht
                par = idx % 2
                wd = wd2[:, par * WD:(par + 1) * WD]
                lwd = p.dma("g", (lambda hh, dst: lambda e: e.dma_start(
                    out=dst, in_=b.d["w_down"][hh], max_dma_last_dim=2048))(ht, wd),
                    deps=[wd_free[par]])
                pk = pp[ht % 2]
                tm = None
                for it in range(64):
                    dep0 = []
                    if it == 0:
                        dep0 = [lwd, lfA, pp_free[ht % 2]]
                    elif it == 32:
                        dep0 = [lfB]
                    tm = p.t((lambda itt, pkk, wdd, ff: lambda e: e.matmul(
                        pkk[:], wdd[:, itt * 128:(itt + 1) * 128],
                        ff[:, itt * SUP:(itt + 1) * SUP],
                        start=(itt == 0), stop=(itt == 63)))(it, pk, wd, fb),
                        deps=dep0, ms=(it == 63))
                wd_free[par] = tm
                last_mm = tm
                tg = p.s((lambda hh, pkk: lambda e: e.activation(
                    gti[:], pkk[:], AF.Silu,
                    scale=b.sb["s_down"][:, hh:hh + 1]))(ht, pk),
                    deps=[tm, g_free], ms=True)
                pp_free[ht % 2] = tg
                tp = p.v((lambda hh: lambda e: e.tensor_tensor(
                    pti[:], gti[:], s1b[:, hh * SUP:(hh + 1) * SUP], op=ALU.mult))(ht),
                    deps=[tg, ls1, p_free], ms=True)
                g_free = tp
                x_done = p.v((lambda hh: lambda e: e.tensor_tensor(
                    xfb[:, hh * SUP:(hh + 1) * SUP], xfb[:, hh * SUP:(hh + 1) * SUP],
                    pti[:], op=ALU.add))(ht), deps=[tp, lxf], ms=True)
                p_free = x_done
            st_y = p.dma("s", (lambda ss: lambda e: e.dma_start(
                out=b.d["y_out"][ss], in_=xfb[:]))(s), deps=[x_done])
            fb_use[0] = last_mm
            use = [x_done, st_y]
    p.barrier()


PHASES = {
    "P0": emit_P0,
    "P1": emit_P1,
    "P2": emit_P2,
    "P3": emit_P3,
    "P4": emit_P4,
    "P5": emit_P5,
    "P6": emit_P6,
    "P7": emit_P7,
}


def build_program(phases=("P0", "P1", "P2", "P3"), dbg=()):
    from contextlib import ExitStack
    b = Build(dbg=dbg)
    declare_tensors(b)
    with ExitStack() as ctx:
        alloc_persistent(b, ctx)
        b.p.barrier()
        for ph in phases:
            PHASES[ph](b, ctx)
        finalize(b)
    return b


def finalize(b: Build):
    nc = b.nc
    p = b.p
    from contextlib import ExitStack
    with ExitStack() as sctx:
        sems = {s: sctx.enter_context(nc.semaphore(f"sem_{s}")) for s in SEMS}
        with nc.Block() as block:
            def playback(stream, eng):
                for ent in stream:
                    if ent[0] == "w":
                        eng.wait_ge(sems[ent[1]], ent[2])
                    else:
                        inst = ent[1](eng)
                        for (sname, n) in ent[2]:
                            inst.then_inc(sems[sname], n)

            @block.tensor
            def _(eng):
                playback(p.streams["t"], eng)

            @block.scalar
            def _(eng):
                playback(p.streams["s"], eng)

            @block.vector
            def _(eng):
                playback(p.streams["v"], eng)

            @block.sync
            def _(eng):
                playback(p.streams["y"], eng)

            @block.gpsimd
            def _(eng):
                playback(p.streams["g"], eng)
                eng.wait_ge(sems["SD"], p.counts["SD"])
                eng.wait_ge(sems["Y"], p.counts["Y"])



def prep_inputs(**inp):
    f = {k: np.asarray(v) for k, v in inp.items()}
    cosT, sinT = _rope_tables()
    masks = _build_masks()
    shared = {
        "WuT": _sbufify(np.ascontiguousarray(f["L0_attn_Wu"].T)),
        "WvT": _sbufify(np.ascontiguousarray(f["L0_attn_Wv"].T)),
        "Wg1T": _sbufify(np.ascontiguousarray(f["L0_attn_Wg1"].T)),
        "Wg2T": _sbufify(np.ascontiguousarray(f["L0_attn_Wg2"].T)),
        "w0T": _sbufify(np.ascontiguousarray(f["L0_ffn_weight"].T)),
        "bu": _pcol(f["L0_attn_bu"]), "bv": _pcol(f["L0_attn_bv"]),
        "bg1": _pcol(f["L0_attn_bg1"]), "bg2": _pcol(f["L0_attn_bg2"]),
        "convw": np.ascontiguousarray(
            f["L0_attn_conv_w"].reshape(D, 3).reshape(NDT, 128, 3)
            .transpose(1, 0, 2).reshape(128, 48)),
        "convb": _pcol(f["L0_attn_conv_b"]),
        "b0": _pcol(f["L0_ffn_bias"]),
        "WqT": _sbufify(np.ascontiguousarray(f["L1_attn_Wq"].T)),
        "WkT": _sbufify(np.ascontiguousarray(f["L1_attn_Wk"].T)),
        "WvT1": _sbufify(np.ascontiguousarray(f["L1_attn_Wv"].T)),
        "WoT": _sbufify(np.ascontiguousarray(f["L1_attn_Wo"].T)),
        "bo": _pcol(f["L1_attn_bo"]),
        "qnw": _pcol(f["L1_attn_q_norm_w"]), "knw": _pcol(f["L1_attn_k_norm_w"]),
        "w_gate": np.ascontiguousarray(
            f["L1_ffn_w_gate"].reshape(NDT, 128, 8, 1024).transpose(2, 1, 0, 3)
            .reshape(8, 128, NDT * 1024)),
        "w_up": np.ascontiguousarray(
            f["L1_ffn_w_up"].reshape(NDT, 128, 8, 1024).transpose(2, 1, 0, 3)
            .reshape(8, 128, NDT * 1024)),
        "w_down": np.ascontiguousarray(
            f["L1_ffn_w_down"].reshape(64, 128, NDT, 128).transpose(2, 1, 0, 3)
            .reshape(NDT, 128, 64 * 128)),
        "s_gate": np.ascontiguousarray(f["L1_ffn_s_gate"].reshape(64, 128).T),
        "s_up": np.ascontiguousarray(f["L1_ffn_s_up"].reshape(64, 128).T),
        "s_down": _pcol(f["L1_ffn_s_down"]),
        "ones_col": np.ones((128, 1), dtype=BF),
        "ones_row": np.ones((1, 128), dtype=BF),
        "perm128": np.eye(128, dtype=BF)[:, list(range(64, 128)) + list(range(64))],
        "cosT": cosT, "sinT": sinT, "masks": masks,
    }
    in_maps = []
    x = f["L0_attn_x"]
    for c in range(8):
        m = dict(shared)
        m["x"] = _xfm_x(np.asarray(x[c]))
        in_maps.append(m)
    return in_maps


def kernel(**inp):
    in_maps = prep_inputs(**inp)
    b = build_program(phases=tuple(PHASES.keys()))
    res = run_bass_kernel_spmd(b.nc, in_maps, core_ids=list(range(8)))
    out = np.empty((8, T, D), dtype=np.float32)
    for c in range(8):
        out[c] = _inv_x(np.asarray(res.results[c]["y_out"], dtype=np.float32))
    return out


# version 09
# speedup vs baseline: 1.0552x; 1.0552x over previous
import numpy as np
import ml_dtypes
import concourse.bass as bass
import concourse.mybir as mybir
from concourse.bass_utils import run_bass_kernel_spmd

F32 = mybir.dt.float32
BF16 = mybir.dt.bfloat16
I8 = mybir.dt.int8
AF = mybir.ActivationFunctionType
ALU = mybir.AluOpType
BF = ml_dtypes.bfloat16

D = 2048
T = 4096
NSUP = 8
SUP = 512
NDT = 16
HD = 512
INTER = 8192
NHEAD = 16
NKV = 4
DHEAD = 128
EPS_LN = 1e-5
EPS_RMS = 1e-6
SOFT_CAP = 50.0


def _sbufify(m):
    R, C = m.shape
    n = R // 128
    return np.ascontiguousarray(m.reshape(n, 128, C).transpose(1, 0, 2).reshape(128, n * C))


def _pcol(v):
    n = v.shape[0] // 128
    return np.ascontiguousarray(v.reshape(n, 128).T)


def _xfm_x(xb):
    return np.ascontiguousarray(
        xb.reshape(NSUP, SUP, NDT, 128).transpose(0, 3, 2, 1).reshape(NSUP, 128, NDT * SUP))


def _inv_x(y):
    return np.ascontiguousarray(
        y.reshape(NSUP, 128, NDT, SUP).transpose(0, 3, 2, 1).reshape(T, D))


def _rope_tables():
    inv_freq = (1.0 / 10000.0 ** (np.arange(0, DHEAD, 2, dtype=np.float64) / DHEAD))
    t = np.arange(T, dtype=np.float64)
    freqs = np.outer(t, inv_freq)
    return (np.cos(freqs).T.astype(np.float32).copy(),
            np.sin(freqs).T.astype(np.float32).copy())


def _build_masks():
    masks = np.zeros((4, 128, SUP), dtype=np.float32)
    ki = np.arange(128)[:, None]
    qi = np.arange(128)[None, :]
    for m in range(4):
        for j in range(4):
            blk = masks[m][:, j * 128:(j + 1) * 128]
            if j < m:
                blk[:] = -1e9
            elif j == m:
                blk[:] = np.where(ki <= qi, 0.0, -1e9)
    return np.ascontiguousarray(masks.transpose(1, 0, 2).reshape(128, 4 * SUP))



class Tok:
    __slots__ = ("sem", "val")

    def __init__(self, sem, val):
        self.sem = sem
        self.val = val


_OWN = {"t": "T", "s": "S", "v": "V", "y": None, "g": None}
SEMS = ["T", "S", "V", "Y", "G", "SD"]


class Prog:
    def __init__(self):
        self.streams = {e: [] for e in "tsvyg"}
        self.counts = {s: 0 for s in SEMS}
        self.lastwait = {e: {} for e in "tsvyg"}
        self.active = {e: None for e in "tsvyg"}

    def op(self, e, fn, deps=(), sem=None, inc=0, sdeps=()):
        st = self.streams[e]
        for tok in deps:
            if tok is None:
                continue
            if _OWN[e] == tok.sem:
                continue
            lw = self.lastwait[e]
            if lw.get(tok.sem, 0) >= tok.val:
                continue
            lw[tok.sem] = tok.val
            st.append(("w", tok.sem, tok.val))
        for tok in sdeps:
            if tok is None:
                continue
            lw = self.lastwait[e]
            if lw.get(tok.sem, 0) >= tok.val:
                continue
            lw[tok.sem] = tok.val
            st.append(("w", tok.sem, tok.val))
        incs = []
        if inc:
            self.counts[sem] += inc
            incs.append((sem, inc))
        ent = ["o", fn, incs]
        st.append(ent)
        self.active[e] = ent
        return Tok(sem, self.counts[sem]) if inc else None

    def t(self, fn, deps=(), ms=False, sdeps=()):
        return self.op("t", fn, deps, "T", 1 if ms else 0, sdeps)

    def s(self, fn, deps=(), ms=False, sdeps=()):
        return self.op("s", fn, deps, "S", 1 if ms else 0, sdeps)

    def v(self, fn, deps=(), ms=False, sdeps=()):
        return self.op("v", fn, deps, "V", 1 if ms else 0, sdeps)

    def dma(self, e, fn, deps=()):
        sem = {"y": "Y", "g": "G", "s": "SD"}[e]
        return self.op(e, fn, deps, sem, 16)

    def barrier(self):
        toks = []
        for e in "tsvyg":
            ent = self.active[e]
            if ent is None:
                continue
            if not ent[2]:
                sem = _OWN[e] or ("Y" if e == "y" else "G")
                amt = 1 if _OWN[e] else 16
                self.counts[sem] += amt
                ent[2].append((sem, amt))
            sem, _ = ent[2][-1]
            toks.append((e, Tok(sem, self.counts[sem])))
        for e in "tsvyg":
            st = self.streams[e]
            for (pe, tok) in toks:
                if pe == e or _OWN[e] == tok.sem:
                    continue
                lw = self.lastwait[e]
                if lw.get(tok.sem, 0) >= tok.val:
                    continue
                lw[tok.sem] = tok.val
                st.append(("w", tok.sem, tok.val))
            self.active[e] = None



class Build:

    def __init__(self, dbg=()):
        self.dbg = set(dbg)
        self.nc = bass.Bass("TRN2", target_bir_lowering=False)
        self.p = Prog()
        self.d = {}
        self.sb = {}
        self.ctx = None

    def dram_in(self, name, shape, dtype):
        self.d[name] = self.nc.dram_tensor(name, shape, dtype, kind="ExternalInput").ap()
        return self.d[name]

    def dram_scratch(self, name, shape, dtype):
        kind = "ExternalOutput" if name in self.dbg else "Internal"
        self.d[name] = self.nc.dram_tensor(name, shape, dtype, kind=kind).ap()
        return self.d[name]

    def dram_out(self, name, shape, dtype):
        self.d[name] = self.nc.dram_tensor(name, shape, dtype, kind="ExternalOutput").ap()
        return self.d[name]


def declare_tensors(b: Build):
    b.dram_in("x", [NSUP, 128, NDT * SUP], F32)
    b.dram_out("y_out", [NSUP, 128, NDT * SUP], F32)
    b.dram_in("WuT", [128, NDT * D], F32)
    b.dram_in("WvT", [128, NDT * D], F32)
    b.dram_in("Wg1T", [128, NDT * HD], F32)
    b.dram_in("Wg2T", [128, 4 * D], F32)
    b.dram_in("w0T", [128, NDT * D], F32)
    b.dram_in("bu", [128, 16], F32)
    b.dram_in("bv", [128, 16], F32)
    b.dram_in("bg1", [128, 4], F32)
    b.dram_in("bg2", [128, 16], F32)
    b.dram_in("convw", [128, 48], F32)
    b.dram_in("convb", [128, 16], F32)
    b.dram_in("b0", [128, 16], F32)
    b.dram_in("WqT", [128, NDT * D], F32)
    b.dram_in("WkT", [128, NDT * 512], F32)
    b.dram_in("WvT1", [128, NDT * 512], F32)
    b.dram_in("WoT", [128, NDT * D], F32)
    b.dram_in("bo", [128, 16], F32)
    b.dram_in("qnw", [128, 1], F32)
    b.dram_in("knw", [128, 1], F32)
    b.dram_in("w_gate", [8, 128, NDT * 1024], I8)
    b.dram_in("w_up", [8, 128, NDT * 1024], I8)
    b.dram_in("w_down", [16, 128, 64 * 128], I8)
    b.dram_in("s_gate", [128, 64], F32)
    b.dram_in("s_up", [128, 64], F32)
    b.dram_in("s_down", [128, 16], F32)
    b.dram_in("ones_col", [128, 1], BF16)
    b.dram_in("ones_row", [1, 128], BF16)
    b.dram_in("perm128", [128, 128], BF16)
    b.dram_in("cosT", [64, T], F32)
    b.dram_in("sinT", [64, T], F32)
    b.dram_in("masks", [128, 4 * SUP], F32)
    b.dram_scratch("bwT_s", [128, NDT * D], BF16)
    b.dram_scratch("xnT0_s", [NSUP, 128, NDT * SUP], BF16)
    b.dram_scratch("gatesT_s", [NSUP, 128, NDT * SUP], BF16)
    b.dram_scratch("kcT_s", [NSUP, 128, NDT * SUP], BF16)
    b.dram_scratch("ugT_s", [NSUP, 128, NDT * SUP], BF16)
    b.dram_scratch("vgT_s", [NSUP, 128, NDT * SUP], BF16)
    b.dram_scratch("goT_s", [NSUP, 128, NDT * SUP], BF16)
    b.dram_scratch("xT1_s", [NSUP, 128, NDT * SUP], F32)
    b.dram_scratch("sT0_s", [NSUP, 128, NDT * SUP], BF16)
    b.dram_scratch("gT0_s", [NSUP, 128, NDT * SUP], BF16)
    b.dram_scratch("xnT1_s", [NSUP, 128, NDT * SUP], BF16)
    b.dram_scratch("qT_s", [NSUP, 128, NHEAD * SUP], BF16)
    b.dram_scratch("kT_s", [128, NKV * T], BF16)
    b.dram_scratch("vTok_s", [128, 32 * 512], BF16)
    b.dram_scratch("sT1_s", [NSUP, 128, NDT * SUP], BF16)
    b.dram_scratch("fuseT_s", [NSUP, 128, 64 * SUP], BF16)
    b.dram_scratch("wd_bf_s", [16, 128, 64 * 128], BF16)


def alloc_persistent(b: Build, ctx):
    nc = b.nc
    names = [
        ("bu", [128, 16], F32), ("bv", [128, 16], F32), ("bg1", [128, 4], F32),
        ("bg2", [128, 16], F32), ("convw", [128, 48], F32), ("convb", [128, 16], F32),
        ("b0", [128, 16], F32), ("bo", [128, 16], F32),
        ("qnw", [128, 1], F32), ("knw", [128, 1], F32),
        ("s_gate", [128, 64], F32), ("s_up", [128, 64], F32), ("s_down", [128, 16], F32),
        ("ones_col", [128, 1], BF16), ("ones_row", [1, 128], BF16),
        ("perm128", [128, 128], BF16),
    ]
    for (n, shape, dt) in names:
        b.sb[n] = ctx.enter_context(nc.sbuf_tensor(f"c_{n}", shape, dt))
    for (n, shape, dt) in names:
        b.p.dma("y", (lambda nn: lambda e: e.dma_start(out=b.sb[nn][:], in_=b.d[nn]))(n))
    b.sb["carry"] = ctx.enter_context(nc.sbuf_tensor("carry", [128, 16], F32))
    b.sb["zeros"] = ctx.enter_context(nc.sbuf_tensor("zeros", [128, SUP], BF16))
    b.p.v(lambda e: e.memset(b.sb["zeros"][:], 0.0))
    b.p.v(lambda e: e.memset(b.sb["carry"][:], 0.0))
    b.sb["halo"] = ctx.enter_context(nc.sbuf_tensor("halo", [128, 2 * NSUP * NDT], BF16))
    b.cc = {}
    ccvals = [EPS_LN, -0.5 * float(np.log(128.0)), -25.0]
    b.sb["ccols"] = ctx.enter_context(nc.sbuf_tensor("ccols", [128, len(ccvals)], F32))
    for i, vv in enumerate(ccvals):
        b.cc[vv] = b.sb["ccols"][:, i:i + 1]
        b.p.v((lambda ii, v2: lambda e: e.memset(b.sb["ccols"][:, ii:ii + 1], v2))(i, vv))


def mask_ap(b, m):
    return b.sb["masks"][:, m * SUP:(m + 1) * SUP]


def ln_stats_and_norm(b, ctx, p, xbf, xnb, pr, pb, tag):
    nc = b.nc
    toks_sx = None
    for j in range(NDT):
        toks_sx = p.t((lambda jj: lambda e: e.matmul(
            pr["sx"][0:1, :], b.sb["ones_col"][:], xbf[:, jj * SUP:(jj + 1) * SUP],
            start=(jj == 0), stop=(jj == NDT - 1)))(j),
            deps=[pr["sx_free"], pr["ld"]] if j == 0 else (), ms=(j == NDT - 1))
    sq_toks = []
    mm2_toks = [None, None]
    toks_sx2 = None
    for j in range(NDT):
        dep0 = [mm2_toks[j % 2]]
        if j == 0:
            dep0 += [pr["ld"], pr["x2_free"]]
        sq_toks.append(p.s((lambda jj: lambda e: e.activation(
            pb["x2"][:, (jj % 2) * SUP:(jj % 2 + 1) * SUP],
            xbf[:, jj * SUP:(jj + 1) * SUP], AF.Square))(j),
            deps=dep0, ms=True))
        toks_sx2 = p.t((lambda jj: lambda e: e.matmul(
            pr["sx2"][0:1, :], b.sb["ones_col"][:],
            pb["x2"][:, (jj % 2) * SUP:(jj % 2 + 1) * SUP],
            start=(jj == 0), stop=(jj == NDT - 1)))(j),
            deps=[sq_toks[j], pr["sx2_free"]], ms=True)
        mm2_toks[j % 2] = toks_sx2
    pr["x2_free"] = toks_sx2
    t_m = p.s(lambda e: e.activation(pb["mrow"][0:1, :], pr["sx"][0:1, :], AF.Copy,
                                     scale=1.0 / D), deps=[toks_sx], ms=True)
    t_msq = p.v(lambda e: e.tensor_tensor(pb["msq"][0:1, :], pb["mrow"][0:1, :],
                                          pb["mrow"][0:1, :], op=ALU.mult),
                deps=[t_m], ms=True)
    t_var = p.v(lambda e: e.scalar_tensor_tensor(
        pb["var"][0:1, :], pr["sx2"][0:1, :], 1.0 / D, pb["msq"][0:1, :],
        op0=ALU.mult, op1=ALU.subtract), deps=[toks_sx2, t_msq], ms=True)
    t_ln = p.s(lambda e: e.activation(pb["var"][0:1, :], pb["var"][0:1, :], AF.Ln,
                                      bias=b.cc[EPS_LN][0:1, :]), deps=[t_var], ms=True)
    t_a = p.s(lambda e: e.activation(pb["arow"][0:1, :], pb["var"][0:1, :], AF.Exp,
                                     scale=-0.5), deps=[t_ln], ms=True)
    t_b1 = p.v(lambda e: e.tensor_tensor(pb["brow_f"][0:1, :], pb["mrow"][0:1, :],
                                         pb["arow"][0:1, :], op=ALU.mult),
               deps=[t_a], ms=True)
    t_b = p.v(lambda e: e.tensor_scalar(pb["brow"][0:1, :], pb["brow_f"][0:1, :],
                                        -1.0, None, op0=ALU.mult), deps=[t_b1], ms=True)
    t_abc = p.t(lambda e: e.matmul(pr["abc"][:], b.sb["ones_row"][:], pb["arow"][0:1, :],
                                   start=True, stop=True),
                deps=[t_a, pr["abc_free"]], ms=True)
    t_bbc = p.t(lambda e: e.matmul(pr["bbc"][:], b.sb["ones_row"][:], pb["brow"][0:1, :],
                                   start=True, stop=True),
                deps=[t_b, pr["bbc_free"]], ms=True)
    t_ae = p.s(lambda e: e.activation(pb["abc_sb"][:], pr["abc"][:], AF.Copy),
               deps=[t_abc], ms=True)
    t_be = p.s(lambda e: e.activation(pb["bbc_sb"][:], pr["bbc"][:], AF.Copy),
               deps=[t_bbc], ms=True)
    pr["sx_free"] = t_m
    pr["sx2_free"] = t_var
    pr["abc_free"] = t_ae
    pr["bbc_free"] = t_be
    tok = None
    xn_free = pr.get("xn_free") or []
    for j in range(NDT):
        p.v((lambda jj: lambda e: e.tensor_tensor(
            xnb[:, jj * SUP:(jj + 1) * SUP], xbf[:, jj * SUP:(jj + 1) * SUP],
            pb["abc_sb"][:], op=ALU.mult))(j),
            deps=[t_ae] + (list(xn_free) if j == 0 else []))
        tok = p.v((lambda jj: lambda e: e.tensor_tensor(
            xnb[:, jj * SUP:(jj + 1) * SUP], xnb[:, jj * SUP:(jj + 1) * SUP],
            pb["bbc_sb"][:], op=ALU.add))(j), deps=[t_be], ms=(j == NDT - 1))
    return tok


def emit_P0(b: Build, ctx):
    nc, p = b.nc, b.p
    with (
        nc.sbuf_tensor("p0_w", [128, NDT * D], BF16) as wbuf,
        nc.sbuf_tensor("p0_abs", [128, NDT * 512], BF16) as absb,
        nc.sbuf_tensor("p0_arow", [1, D], BF16) as arow,
        nc.sbuf_tensor("p0_abc", [128, D], BF16) as abc,
        nc.sbuf_tensor("p0_t1", [128, D], F32) as t1,
        nc.psum_tensor("p0_ps", [128, 512], F32) as ps,
        nc.psum_tensor("p0_ps2", [128, 512], F32) as ps2,
    ):
        ld = p.dma("g", lambda e: e.dma_start(out=wbuf[:], in_=b.d["w0T"],
                                              max_dma_last_dim=8192))
        tar = tae = None
        for oc in range(4):
            ab_toks = []
            for j in range(NDT):
                ab_toks.append(p.s((lambda jj, occ: lambda e: e.activation(
                    absb[:, jj * 512:(jj + 1) * 512],
                    wbuf[:, jj * D + occ * 512: jj * D + (occ + 1) * 512], AF.Abs))(j, oc),
                    deps=[ld], ms=True))
            tma = None
            for j in range(NDT):
                tma = p.t((lambda jj, occ: lambda e: e.matmul(
                    ps[0:1, :], b.sb["ones_col"][:], absb[:, jj * 512:(jj + 1) * 512],
                    start=(jj == 0), stop=(jj == NDT - 1)))(j, oc),
                    deps=[ab_toks[j], tar] if j == 0 else [ab_toks[j]],
                    ms=(j == NDT - 1))
            tar = p.s((lambda occ: lambda e: e.activation(
                arow[0:1, occ * 512:(occ + 1) * 512], ps[0:1, :], AF.Copy,
                scale=1.0 / D))(oc), deps=[tma], ms=True)
            tab = p.t((lambda occ: lambda e: e.matmul(
                ps2[:], b.sb["ones_row"][:], arow[0:1, occ * 512:(occ + 1) * 512],
                start=True, stop=True))(oc), deps=[tar, tae], ms=True)
            tae = p.s((lambda occ: lambda e: e.activation(
                abc[:, occ * 512:(occ + 1) * 512], ps2[:], AF.Copy))(oc),
                deps=[tab], ms=True)
        st_tok = None
        for j in range(NDT):
            tg = p.v((lambda jj: lambda e: e.tensor_scalar(
                t1[:], wbuf[:, jj * D:(jj + 1) * D], 0.0, 2.0,
                op0=ALU.is_ge, op1=ALU.mult))(j), deps=[tae], ms=True)
            p.v((lambda jj: lambda e: e.tensor_tensor(
                t1[:], t1[:], abc[:], op=ALU.mult))(j))
            tb = p.v((lambda jj: lambda e: e.tensor_tensor(
                wbuf[:, jj * D:(jj + 1) * D], t1[:], abc[:], op=ALU.subtract))(j), ms=True)
            st_tok = tb
        p.dma("s", lambda e: e.dma_start(out=b.d["bwT_s"], in_=wbuf[:]), deps=[st_tok])
    p.barrier()


def emit_P1(b: Build, ctx):
    nc, p = b.nc, b.p
    with (
        nc.sbuf_tensor("p1_wg1", [128, NDT * HD], BF16) as wg1,
        nc.sbuf_tensor("p1_wg2", [128, 4 * D], BF16) as wg2,
        nc.sbuf_tensor("p1_xbf", [128, 2 * NDT * SUP], BF16) as xbf2,
        nc.sbuf_tensor("p1_xn", [128, NDT * SUP], BF16) as xnb,
        nc.sbuf_tensor("p1_x2", [128, 2 * SUP], BF16) as x2b,
        nc.sbuf_tensor("p1_h", [128, 4 * SUP], BF16) as hb,
        nc.sbuf_tensor("p1_g", [128, NDT * SUP], BF16) as gb,
        nc.sbuf_tensor("p1_rows", [1, 8 * SUP], F32) as rows,
        nc.psum_tensor("p1_pa", [128, 512], F32) as pa,
        nc.psum_tensor("p1_pb", [128, 512], F32) as pbk,
        nc.psum_tensor("p1_sx", [128, 512], F32) as psx,
        nc.psum_tensor("p1_sx2", [128, 512], F32) as psx2,
        nc.psum_tensor("p1_abc", [128, 512], F32) as pabc,
        nc.psum_tensor("p1_bbc", [128, 512], F32) as pbbc,
    ):
        lw1 = p.dma("g", lambda e: e.dma_start(out=wg1[:], in_=b.d["Wg1T"],
                                               max_dma_last_dim=8192))
        lw2 = p.dma("g", lambda e: e.dma_start(out=wg2[:], in_=b.d["Wg2T"],
                                               max_dma_last_dim=8192))
        pr = {"sx": psx, "sx2": psx2, "abc": pabc, "bbc": pbbc,
              "sx_free": None, "sx2_free": None, "abc_free": None, "bbc_free": None,
              "x2_free": None, "xn_free": None}
        pb = {"x2": x2b,
              "mrow": rows[:, 0 * SUP:1 * SUP], "msq": rows[:, 1 * SUP:2 * SUP],
              "var": rows[:, 2 * SUP:3 * SUP], "arow": None, "brow_f": rows[:, 4 * SUP:5 * SUP],
              "brow": None, "abc_sb": None, "bbc_sb": None}
        with (
            nc.sbuf_tensor("p1_arow", [1, SUP], BF16) as arow_t,
            nc.sbuf_tensor("p1_brow", [1, SUP], BF16) as brow_t,
            nc.sbuf_tensor("p1_absb", [128, SUP], BF16) as abc_sb,
            nc.sbuf_tensor("p1_bbsb", [128, SUP], BF16) as bbc_sb,
        ):
            pb["arow"] = arow_t
            pb["brow"] = brow_t
            pb["abc_sb"] = abc_sb
            pb["bbc_sb"] = bbc_sb
            xbf_use = [None, None]
            pp = [pa, pbk]
            pp_free = [None, None]
            g_spill = None
            for s in range(NSUP):
                xbf = xbf2[:, (s % 2) * NDT * SUP:(s % 2 + 1) * NDT * SUP]
                ldx = p.dma("g", (lambda ss, dst: lambda e: e.dma_start(
                    out=dst, in_=b.d["x"][ss], max_dma_last_dim=8192))(s, xbf),
                    deps=[xbf_use[s % 2]])
                pr["ld"] = ldx
                tok_xn = ln_stats_and_norm(b, ctx, p, xbf, xnb, pr, pb, f"p1s{s}")
                xbf_use[s % 2] = tok_xn
                st_xn = p.dma("s", (lambda ss: lambda e: e.dma_start(
                    out=b.d["xnT0_s"][ss], in_=xnb[:]))(s), deps=[tok_xn])
                p.v((lambda ss: lambda e: e.tensor_copy(
                    b.sb["halo"][:, ss * NDT:(ss + 1) * NDT],
                    xnb[:, 0:NDT * SUP:SUP]))(s))
                p.v((lambda ss: lambda e: e.tensor_copy(
                    b.sb["halo"][:, NSUP * NDT + ss * NDT:NSUP * NDT + (ss + 1) * NDT],
                    xnb[:, SUP - 1:NDT * SUP:SUP]))(s))
                h_toks = []
                tm_h_last = None
                for ht in range(4):
                    pk = pp[ht % 2]
                    tm = None
                    for j in range(NDT):
                        tm = p.t((lambda jj, hh, pkk: lambda e: e.matmul(
                            pkk[:], wg1[:, jj * HD + hh * 128: jj * HD + (hh + 1) * 128],
                            xnb[:, jj * SUP:(jj + 1) * SUP],
                            start=(jj == 0), stop=(jj == NDT - 1)))(j, ht, pk),
                            deps=([lw1, tok_xn, pp_free[ht % 2]] if j == 0 else ()),
                            ms=(j == NDT - 1))
                    te = p.s((lambda hh, pkk: lambda e: e.activation(
                        hb[:, hh * SUP:(hh + 1) * SUP], pkk[:], AF.Relu,
                        bias=b.sb["bg1"][:, hh:hh + 1]))(ht, pk), deps=[tm], ms=True)
                    pp_free[ht % 2] = te
                    h_toks.append(te)
                    tm_h_last = tm
                pr["xn_free"] = [tm_h_last, st_xn]
                g_tok = None
                for rt in range(NDT):
                    pk = pp[rt % 2]
                    tm = None
                    for ht in range(4):
                        tm = p.t((lambda hh, rr, pkk: lambda e: e.matmul(
                            pkk[:], wg2[:, hh * D + rr * 128: hh * D + (rr + 1) * 128],
                            hb[:, hh * SUP:(hh + 1) * SUP],
                            start=(hh == 0), stop=(hh == 3)))(ht, rt, pk),
                            deps=[lw2, h_toks[ht]] + ([pp_free[rt % 2]] if ht == 0 else []),
                            ms=(ht == 3))
                    te = p.s((lambda rr, pkk: lambda e: e.activation(
                        gb[:, rr * SUP:(rr + 1) * SUP], pkk[:], AF.Sigmoid,
                        bias=b.sb["bg2"][:, rr:rr + 1]))(rt, pk),
                        deps=[tm] + ([g_spill] if rt == 0 else []), ms=True)
                    pp_free[rt % 2] = te
                    g_tok = te
                g_spill = p.dma("s", (lambda ss: lambda e: e.dma_start(
                    out=b.d["gatesT_s"][ss], in_=gb[:]))(s), deps=[g_tok])
    p.barrier()


def _emit_uv_pass(b, which, ctx):
    nc, p = b.nc, b.p
    wkey = "WuT" if which == "u" else "WvT"
    with (
        nc.sbuf_tensor(f"p2{which}_w", [128, NDT * D], BF16) as ww,
        nc.sbuf_tensor(f"p2{which}_xn", [128, 2 * NDT * SUP], BF16) as xn2,
        nc.sbuf_tensor(f"p2{which}_gt", [128, 2 * NDT * SUP], BF16) as gt2,
        nc.sbuf_tensor(f"p2{which}_t", [128, NDT * SUP], BF16) as tb,
        nc.sbuf_tensor(f"p2{which}_o", [128, NDT * SUP], BF16) as ob,
        nc.sbuf_tensor(f"p2{which}_kc", [128, 2 * NDT * SUP] if which == "v" else [128, 1], BF16) as kc2,
        nc.psum_tensor(f"p2{which}_pa", [128, 512], F32) as pa,
        nc.psum_tensor(f"p2{which}_pb", [128, 512], F32) as pbk,
    ):
        lw = p.dma("g", (lambda wk: lambda e: e.dma_start(
            out=ww[:], in_=b.d[wk], max_dma_last_dim=8192))(wkey))
        bias = b.sb["bu"] if which == "u" else b.sb["bv"]
        use = [None, None]
        pp = [pa, pbk]
        pp_free = [None, None]
        o_st = None
        tb_free = None
        carry_cp = [None] * NDT
        for s in range(NSUP):
            W = NDT * SUP
            par = s % 2
            xn = xn2[:, par * W:(par + 1) * W]
            gt = gt2[:, par * W:(par + 1) * W]
            lx = p.dma("y", (lambda ss, dst: lambda e: e.dma_start(
                out=dst, in_=b.d["xnT0_s"][ss]))(s, xn), deps=[use[s % 2]])
            lg = p.dma("y", (lambda ss, dst: lambda e: e.dma_start(
                out=dst, in_=b.d["gatesT_s"][ss]))(s, gt), deps=[use[s % 2]])
            if which == "v":
                kc = kc2[:, (s % 2) * W:(s % 2 + 1) * W]
                lkc = p.dma("y", (lambda ss, dst: lambda e: e.dma_start(
                    out=dst, in_=b.d["kcT_s"][ss]))(s, kc), deps=[use[s % 2]])
            last_v = None
            for rt in range(NDT):
                pk = pp[rt % 2]
                tm = None
                for j in range(NDT):
                    tm = p.t((lambda jj, rr, pkk, xnn: lambda e: e.matmul(
                        pkk[:], ww[:, jj * D + rr * 128: jj * D + (rr + 1) * 128],
                        xnn[:, jj * SUP:(jj + 1) * SUP],
                        start=(jj == 0), stop=(jj == NDT - 1)))(j, rt, pk, xn),
                        deps=([lw, lx, pp_free[rt % 2]] if j == 0 else ()),
                        ms=(j == NDT - 1))
                tg = p.v((lambda rr, pkk, bb, gtt: lambda e: e.scalar_tensor_tensor(
                    tb[:, rr * SUP:(rr + 1) * SUP], pkk[:], bb[:, rr:rr + 1],
                    gtt[:, rr * SUP:(rr + 1) * SUP], op0=ALU.add, op1=ALU.mult))(rt, pk, bias, gt),
                    deps=[tm, lg] + ([tb_free] if rt == 0 else []), ms=True)
                pp_free[rt % 2] = tg
                if which == "u":
                    p.v((lambda rr: lambda e: e.tensor_tensor_scan(
                        ob[:, rr * SUP:(rr + 1) * SUP], tb[:, rr * SUP:(rr + 1) * SUP],
                        b.sb["zeros"][:], 0.0,
                        op0=ALU.add, op1=ALU.add))(rt),
                        deps=[tg] + ([o_st] if rt == 0 else []))
                    tadd = p.v((lambda rr: lambda e: e.tensor_scalar(
                        ob[:, rr * SUP:(rr + 1) * SUP], ob[:, rr * SUP:(rr + 1) * SUP],
                        b.sb["carry"][:, rr:rr + 1], None, op0=ALU.add))(rt),
                        sdeps=[carry_cp[rt]], ms=True)
                    last_v = carry_cp[rt] = p.v((lambda rr: lambda e: e.tensor_copy(
                        b.sb["carry"][:, rr:rr + 1],
                        ob[:, rr * SUP + SUP - 1: rr * SUP + SUP]))(rt),
                        sdeps=[tadd], ms=True)
                else:
                    last_v = p.v((lambda rr, kcc: lambda e: e.tensor_tensor(
                        ob[:, rr * SUP:(rr + 1) * SUP], kcc[:, rr * SUP:(rr + 1) * SUP],
                        tb[:, rr * SUP:(rr + 1) * SUP], op=ALU.mult))(rt, kc),
                        deps=[tg, lkc] + ([o_st] if rt == 0 else []), ms=True)
            use[s % 2] = last_v
            o_st = p.dma("s", (lambda ss, dk: lambda e: e.dma_start(
                out=b.d[dk][ss], in_=ob[:]))(s, "kcT_s" if which == "u" else "goT_s"),
                deps=[last_v])
            if ("ugT_s" if which == "u" else "vgT_s") in b.dbg:
                tb_free = p.dma("s", (lambda ss, dk: lambda e: e.dma_start(
                    out=b.d[dk][ss], in_=tb[:]))(s, "ugT_s" if which == "u" else "vgT_s"),
                    deps=[last_v])
    p.barrier()


def emit_P2a(b: Build, ctx):
    _emit_uv_pass(b, "u", ctx)


def emit_P2b(b: Build, ctx):
    _emit_uv_pass(b, "v", ctx)


def emit_P3(b: Build, ctx):
    nc, p = b.nc, b.p
    with (
        nc.sbuf_tensor("p3_wun", [128, NDT * D], BF16) as wun,
        nc.sbuf_tensor("p3_bw", [128, NDT * D], BF16) as bw,
        nc.sbuf_tensor("p3_xn", [128, NDT * SUP], BF16) as xn2,
        nc.sbuf_tensor("p3_go", [128, NDT * SUP], BF16) as go2,
        nc.sbuf_tensor("p3_x", [128, NDT * SUP], F32) as xf2,
        nc.sbuf_tensor("p3_cv", [128, SUP], F32) as cvb,
        nc.sbuf_tensor("p3_st", [128, SUP], BF16) as stb,
        nc.sbuf_tensor("p3_gt", [128, SUP], BF16) as gtb,
        nc.psum_tensor("p3_pa", [128, 512], F32) as pa,
        nc.psum_tensor("p3_pb", [128, 512], F32) as pbk,
        nc.psum_tensor("p3_pc", [128, 512], F32) as pc,
        nc.psum_tensor("p3_pd", [128, 512], F32) as pd,
    ):
        lwu = p.dma("g", lambda e: e.dma_start(out=wun[:], in_=b.d["WuT"],
                                               max_dma_last_dim=8192))
        lbw = p.dma("y", lambda e: e.dma_start(out=bw[:], in_=b.d["bwT_s"]))
        use = [[]]
        ppa = [pa, pbk]
        ppb = [pc, pd]
        ppa_free = [None, None]
        ppb_free = [None, None]
        gtb_free = None
        for s in range(NSUP):
            W = NDT * SUP
            xn = xn2[:, 0:W]
            go = go2[:, 0:W]
            xf = xf2[:, 0:W]
            lx = p.dma("y", (lambda ss, dst: lambda e: e.dma_start(
                out=dst, in_=b.d["xnT0_s"][ss]))(s, xn), deps=use[0])
            lgo = p.dma("y", (lambda ss, dst: lambda e: e.dma_start(
                out=dst, in_=b.d["goT_s"][ss]))(s, go), deps=use[0])
            lxf = p.dma("y", (lambda ss, dst: lambda e: e.dma_start(
                out=dst, in_=b.d["x"][ss]))(s, xf), deps=use[0])
            lhl = lhr = None
            def halo_left(dd, ss=s):
                if ss == 0:
                    return b.sb["zeros"][:, 0:1]
                return b.sb["halo"][:, NSUP * NDT + (ss - 1) * NDT + dd:
                                    NSUP * NDT + (ss - 1) * NDT + dd + 1]
            def halo_right(dd, ss=s):
                if ss == NSUP - 1:
                    return b.sb["zeros"][:, 0:1]
                return b.sb["halo"][:, (ss + 1) * NDT + dd:(ss + 1) * NDT + dd + 1]
            x_done = None
            for dt in range(NDT):
                pku = ppa[dt % 2]
                pkf = ppb[dt % 2]
                tmu = tmf = None
                for rt in range(NDT):
                    tmu = p.t((lambda rr, dd, pkk, goo: lambda e: e.matmul(
                        pkk[:], wun[:, rr * D + dd * 128: rr * D + (dd + 1) * 128],
                        goo[:, rr * SUP:(rr + 1) * SUP],
                        start=(rr == 0), stop=(rr == NDT - 1)))(rt, dt, pku, go),
                        deps=([lwu, lgo, ppa_free[dt % 2]] if rt == 0 else ()),
                        ms=(rt == NDT - 1))
                for it in range(NDT):
                    tmf = p.t((lambda ii, dd, pkk, xnn: lambda e: e.matmul(
                        pkk[:], bw[:, ii * D + dd * 128: ii * D + (dd + 1) * 128],
                        xnn[:, ii * SUP:(ii + 1) * SUP],
                        start=(ii == 0), stop=(ii == NDT - 1)))(it, dt, pkf, xn),
                        deps=([lbw, lx, ppb_free[dt % 2]] if it == 0 else ()),
                        ms=(it == NDT - 1))
                xs = xn[:, dt * SUP:(dt + 1) * SUP]
                tc0 = p.v((lambda dd, xss: lambda e: e.tensor_scalar(
                    cvb[:, 1:SUP], xss[:, 0:SUP - 1], b.sb["convw"][:, dd * 3:dd * 3 + 1],
                    None, op0=ALU.mult))(dt, xs), deps=[lx], ms=True)
                p.v((lambda dd, apl: lambda e: e.tensor_scalar(
                    cvb[:, 0:1], apl,
                    b.sb["convw"][:, dd * 3:dd * 3 + 1], None,
                    op0=ALU.mult))(dt, halo_left(dt)))
                p.v((lambda dd, xss: lambda e: e.scalar_tensor_tensor(
                    cvb[:], xss[:], b.sb["convw"][:, dd * 3 + 1:dd * 3 + 2], cvb[:],
                    op0=ALU.mult, op1=ALU.add))(dt, xs))
                p.v((lambda dd, xss: lambda e: e.scalar_tensor_tensor(
                    cvb[:, 0:SUP - 1], xss[:, 1:SUP], b.sb["convw"][:, dd * 3 + 2:dd * 3 + 3],
                    cvb[:, 0:SUP - 1], op0=ALU.mult, op1=ALU.add))(dt, xs))
                tcv = p.v((lambda dd, apr: lambda e: e.scalar_tensor_tensor(
                    cvb[:, SUP - 1:SUP], apr,
                    b.sb["convw"][:, dd * 3 + 2:dd * 3 + 3], cvb[:, SUP - 1:SUP],
                    op0=ALU.mult, op1=ALU.add))(dt, halo_right(dt)), ms=True)
                ts = p.v((lambda dd, pkk: lambda e: e.scalar_tensor_tensor(
                    stb[:], cvb[:], b.sb["convb"][:, dd:dd + 1], pkk[:],
                    op0=ALU.add, op1=ALU.add))(dt, pku), deps=[tmu, tcv], ms=True)
                ppa_free[dt % 2] = ts
                tg = p.s((lambda dd, pkk: lambda e: e.activation(
                    gtb[:], pkk[:], AF.Silu, bias=b.sb["b0"][:, dd:dd + 1]))(dt, pkf),
                    deps=[tmf, gtb_free], ms=True)
                ppb_free[dt % 2] = tg
                dbg_deps = []
                if "sT0_s" in b.dbg:
                    dbg_deps.append(p.dma("s", (lambda ss, dd: lambda e: e.dma_start(
                        out=b.d["sT0_s"][ss][:, dd * SUP:(dd + 1) * SUP], in_=stb[:]))(s, dt),
                        deps=[ts]))
                    dbg_deps.append(p.dma("s", (lambda ss, dd: lambda e: e.dma_start(
                        out=b.d["gT0_s"][ss][:, dd * SUP:(dd + 1) * SUP], in_=gtb[:]))(s, dt),
                        deps=[tg]))
                tprod = p.v((lambda dd: lambda e: e.tensor_tensor(
                    stb[:], stb[:], gtb[:], op=ALU.mult))(dt), deps=[tg, ts] + dbg_deps, ms=True)
                gtb_free = tprod
                x_done = p.v((lambda dd, xff: lambda e: e.tensor_tensor(
                    xff[:, dd * SUP:(dd + 1) * SUP], xff[:, dd * SUP:(dd + 1) * SUP],
                    stb[:], op=ALU.add))(dt, xf), deps=[tprod, lxf], ms=True)
            st_x = p.dma("s", (lambda ss, xff: lambda e: e.dma_start(
                out=b.d["xT1_s"][ss], in_=xff))(s, xf), deps=[x_done])
            use[0] = [x_done, st_x]
    p.barrier()




def emit_P4(b: Build, ctx):
    nc, p = b.nc, b.p
    from contextlib import ExitStack
    with ExitStack() as est:
        def al(name, shape, dt):
            return est.enter_context(nc.sbuf_tensor(name, shape, dt))
        def alp(name):
            return est.enter_context(nc.psum_tensor(name, [128, 512], F32))
        wq = al("p4_wq", [128, NDT * D], BF16)
        wk = al("p4_wk", [128, NDT * 512], BF16)
        wv = al("p4_wv", [128, NDT * 512], BF16)
        c2 = al("p4_c2", [128, T], BF16)
        s2 = al("p4_s2", [128, T], BF16)
        xbf = al("p4_xbf", [128, NDT * SUP], BF16)
        xnb = al("p4_xn", [128, NDT * SUP], BF16)
        x2b = al("p4_x2", [128, 2 * SUP], BF16)
        qbuf = al("p4_q", [128, NHEAD * SUP], BF16)
        kbuf = al("p4_k", [128, NKV * SUP], BF16)
        vbuf = al("p4_v", [128, NKV * SUP], BF16)
        rows = al("p4_rows", [1, 6 * SUP], F32)
        arow_t = al("p4_arow", [1, SUP], BF16)
        brow_t = al("p4_brow", [1, SUP], BF16)
        abc_sb = al("p4_absb", [128, SUP], BF16)
        bbc_sb = al("p4_bbsb", [128, SUP], BF16)
        invrow = al("p4_inv", [1, 2 * SUP], BF16)
        qsb = al("p4_qsb", [128, 2 * SUP], BF16)
        qnb = al("p4_qn", [128, SUP], BF16)
        zswb = al("p4_zsw", [128, SUP], BF16)
        m1b = al("p4_m1", [128, SUP], BF16)
        psx = alp("p4_sx")
        psx2 = alp("p4_sx2")
        pabc = alp("p4_abc")
        pbbc = alp("p4_bbc")
        pa = alp("p4_pa")
        pbk = alp("p4_pb")
        prow = alp("p4_row")
        pbc = alp("p4_bc")
        lwq = p.dma("g", lambda e: e.dma_start(out=wq[:], in_=b.d["WqT"],
                                               max_dma_last_dim=8192))
        lwk = p.dma("g", lambda e: e.dma_start(out=wk[:], in_=b.d["WkT"],
                                               max_dma_last_dim=8192))
        lwv = p.dma("g", lambda e: e.dma_start(out=wv[:], in_=b.d["WvT1"],
                                               max_dma_last_dim=8192))
        lc1 = p.dma("g", lambda e: e.dma_start(out=c2[0:64, :], in_=b.d["cosT"],
                                               max_dma_last_dim=8192))
        lc2 = p.dma("g", lambda e: e.dma_start(out=c2[64:128, :], in_=b.d["cosT"],
                                               max_dma_last_dim=8192))
        ls1 = p.dma("g", lambda e: e.dma_start(out=s2[0:64, :], in_=b.d["sinT"],
                                               max_dma_last_dim=8192))
        ls2 = p.dma("g", lambda e: e.dma_start(out=s2[64:128, :], in_=b.d["sinT"],
                                               max_dma_last_dim=8192))
        tneg = p.v(lambda e: e.tensor_scalar(s2[64:128, :], s2[64:128, :], -1.0, None,
                                             op0=ALU.mult), deps=[ls2], ms=True)
        pr = {"sx": psx, "sx2": psx2, "abc": pabc, "bbc": pbbc,
              "sx_free": None, "sx2_free": None, "abc_free": None, "bbc_free": None,
              "x2_free": None, "xn_free": None}
        pb = {"x2": x2b,
              "mrow": rows[:, 0 * SUP:1 * SUP], "msq": rows[:, 1 * SUP:2 * SUP],
              "var": rows[:, 2 * SUP:3 * SUP], "arow": arow_t,
              "brow_f": rows[:, 4 * SUP:5 * SUP],
              "brow": brow_t, "abc_sb": abc_sb, "bbc_sb": bbc_sb}
        rmsrow = rows[:, 5 * SUP:6 * SUP]
        xbf_use = [[]]
        inv_free = None
        inv_bc_free = [None]
        row_free = None
        swap_free = None
        qsb_free = None
        qn_free = None
        zsw_free = None
        st_q = st_k = st_v = st_xn = None
        proj_free = [None, None]
        for s in range(NSUP):
            ldx = p.dma("g", (lambda ss: lambda e: e.dma_start(
                out=xbf[:], in_=b.d["xT1_s"][ss], max_dma_last_dim=8192))(s),
                deps=xbf_use[0])
            pr["ld"] = ldx
            tok_xn = ln_stats_and_norm(b, ctx, p, xbf[:], xnb, pr, pb, f"p4s{s}")
            st_xn = p.dma("s", (lambda ss: lambda e: e.dma_start(
                out=b.d["xnT1_s"][ss], in_=xnb[:]))(s), deps=[tok_xn])
            pp = [pa, pbk]
            pp_free = proj_free
            last_store = None
            qk_plan = [("q", ht, wq, D, b.sb["qnw"], qbuf) for ht in range(NHEAD)] + \
                      [("k", g, wk, 512, b.sb["knw"], kbuf) for g in range(NKV)]

            def emit_proj(idx):
                nonlocal row_free
                kind, ht, wmat, wcols, nw, obuf = qk_plan[idx]
                pk = pp[idx % 2]
                tm = None
                for j in range(NDT):
                    tm = p.t((lambda jj, hh, pkk, wm, wc: lambda e: e.matmul(
                        pkk[:], wm[:, jj * wc + hh * 128: jj * wc + (hh + 1) * 128],
                        xnb[:, jj * SUP:(jj + 1) * SUP],
                        start=(jj == 0), stop=(jj == NDT - 1)))(j, ht, pk, wmat, wcols),
                        deps=([lwq, lwk, tok_xn, pp_free[idx % 2]] if j == 0 else ()),
                        ms=(j == NDT - 1))
                tsq = p.s((lambda pkk: lambda e: e.activation(
                    x2b[:, 0:SUP], pkk[:], AF.Square))(pk),
                    deps=[tm, pr["x2_free"], qsb_free], ms=True)
                tev = p.s((lambda pkk, qs2: lambda e: e.activation(
                    qs2, pkk[:], AF.Copy))(pk, qsb[:, (idx % 2) * SUP:(idx % 2 + 1) * SUP]),
                    ms=True)
                pp_free[idx % 2] = tev
                tssq = p.t(lambda e: e.matmul(prow[0:1, :], b.sb["ones_col"][:],
                                              x2b[:, 0:SUP], start=True, stop=True),
                           deps=[tsq, row_free], ms=True)
                pr["x2_free"] = tssq
                t1r = p.s((lambda rr2: lambda e: e.activation(
                    rr2[0:1, :], prow[0:1, :], AF.Ln))(rmsrow), deps=[tssq], ms=True)
                row_free = t1r
                p.s((lambda rr2: lambda e: e.activation(
                    rr2[0:1, :], rr2[0:1, :], AF.Exp, scale=0.5,
                    bias=b.cc[-0.5 * float(np.log(128.0))][0:1, :]))(rmsrow), ms=True)
                t3r = p.v((lambda rr2: lambda e: e.tensor_scalar(
                    rr2[0:1, :], rr2[0:1, :], EPS_RMS, None, op0=ALU.add))(rmsrow),
                    ms=True)
                p.s((lambda rr2: lambda e: e.activation(
                    rr2[0:1, :], rr2[0:1, :], AF.Ln))(rmsrow), deps=[t3r], ms=True)
                t5r = p.s((lambda iv: lambda e: e.activation(
                    iv, rmsrow[0:1, :], AF.Exp, scale=-1.0))(
                        invrow[0:1, (idx % 2) * SUP:(idx % 2 + 1) * SUP]),
                    deps=[inv_free], ms=True)
                return (idx, tev, t1r, t5r)

            def emit_rope(stg):
                nonlocal inv_free, swap_free, qn_free, zsw_free, qsb_free
                idx, tev, t1r, t5r = stg
                kind, ht, wmat, wcols, nw, obuf = qk_plan[idx]
                tbc = p.t((lambda iv: lambda e: e.matmul(
                    pbc[:], b.sb["ones_row"][:], iv, start=True, stop=True))(
                        invrow[0:1, (idx % 2) * SUP:(idx % 2 + 1) * SUP]),
                    deps=[t5r, inv_bc_free[0]], ms=True)
                inv_free = tbc
                tqn = p.v((lambda nww, qs2: lambda e: e.scalar_tensor_tensor(
                    qnb[:], qs2, nww[:, 0:1], pbc[:], op0=ALU.mult, op1=ALU.mult))(
                        nw, qsb[:, (idx % 2) * SUP:(idx % 2 + 1) * SUP]),
                    deps=[tev, tbc, qn_free], ms=True)
                inv_bc_free[0] = tqn
                tsw = p.t(lambda e: e.matmul(psx[:], b.sb["perm128"][:], qnb[:],
                                             start=True, stop=True),
                          deps=[tqn, swap_free, pr["sx_free"]], ms=True)
                tze = p.s(lambda e: e.activation(zswb[:], psx[:], AF.Copy),
                          deps=[tsw, zsw_free], ms=True)
                swap_free = tze
                pr["sx_free"] = tze
                tm1 = p.v((lambda ss2: lambda e: e.tensor_tensor(
                    m1b[:], qnb[:], c2[:, ss2 * SUP:(ss2 + 1) * SUP], op=ALU.mult))(s),
                    deps=[lc1, lc2], ms=True)
                qn_free = tsw
                tm2 = p.v((lambda ss2, hh, ob: lambda e: e.tensor_tensor(
                    ob[:, hh * SUP:(hh + 1) * SUP], zswb[:],
                    s2[:, ss2 * SUP:(ss2 + 1) * SUP], op=ALU.mult))(s, ht, obuf),
                    deps=[tze, tneg, ls1] + ([st_q, st_k] if idx == 0 else []), ms=True)
                zsw_free = tm2
                tro = p.v((lambda hh, ob: lambda e: e.tensor_tensor(
                    ob[:, hh * SUP:(hh + 1) * SUP], ob[:, hh * SUP:(hh + 1) * SUP],
                    m1b[:], op=ALU.add))(ht, obuf), ms=True)
                qsb_free = tro
                return tro

            pend_h = None
            last_store = None
            for idx in range(len(qk_plan) + 1):
                cur = emit_proj(idx) if idx < len(qk_plan) else None
                if pend_h is not None:
                    last_store = emit_rope(pend_h)
                pend_h = cur
            vev = None
            for sub in range(4):
                pk = pp[sub % 2]
                tm = None
                for j in range(NDT):
                    tm = p.t((lambda jj, sb2, pkk: lambda e: e.matmul(
                        pkk[:], xnb[:, jj * SUP + sb2 * 128: jj * SUP + (sb2 + 1) * 128],
                        wv[:, jj * 512:(jj + 1) * 512],
                        start=(jj == 0), stop=(jj == NDT - 1)))(j, sub, pk),
                        deps=([lwv, tok_xn, pp_free[sub % 2]] if j == 0 else ()),
                        ms=(j == NDT - 1))
                vev = p.s((lambda sb2, pkk: lambda e: e.activation(
                    vbuf[:, sb2 * SUP:(sb2 + 1) * SUP], pkk[:], AF.Copy))(sub, pk),
                    deps=[tm, st_v], ms=True)
                pp_free[sub % 2] = vev
            st_q = p.dma("s", (lambda ss: lambda e: e.dma_start(
                out=b.d["qT_s"][ss], in_=qbuf[:]))(s), deps=[last_store])
            st_k = p.dma("s", (lambda ss: lambda e: e.dma_start(
                out=b.d["kT_s"].rearrange("p (g t) -> p g t", g=NKV)[:, :, ss * SUP:(ss + 1) * SUP],
                in_=kbuf[:].rearrange("p (g t) -> p g t", g=NKV)))(s), deps=[last_store])
            st_v = p.dma("s", (lambda ss: lambda e: e.dma_start(
                out=b.d["vTok_s"][:, ss * 4 * SUP:(ss + 1) * 4 * SUP], in_=vbuf[:]))(s),
                deps=[vev])
            xbf_use[0] = [tok_xn]
            pr["xn_free"] = [tm, st_xn]
    p.barrier()




def emit_P5(b: Build, ctx):
    nc, p = b.nc, b.p
    from contextlib import ExitStack
    import math
    sc_tanh = 1.0 / (SOFT_CAP * math.sqrt(DHEAD))
    with ExitStack() as est:
        def al(name, shape, dt):
            return est.enter_context(nc.sbuf_tensor(name, shape, dt))
        wo = al("p5_wo", [128, NDT * D], BF16)
        kts = al("p5_kt", [128, NKV * T], BF16)
        vts = al("p5_vt", [128, 32 * 512], BF16)
        msk = al("p5_msk", [128, 4 * SUP], F32)
        qts = al("p5_q", [128, NHEAD * SUP], BF16)
        ybuf = al("p5_y", [128, NHEAD * SUP], BF16)
        sbo = al("p5_sbo", [128, NDT * SUP], BF16)
        t1b = al("p5_t1", [128, 2 * 2 * SUP], F32)
        expb = al("p5_exp", [128, 2 * 2 * SUP], BF16)
        denr = al("p5_den", [1, SUP], BF16)
        dbc = al("p5_dbc", [128, SUP], F32)
        pS = [est.enter_context(nc.psum_tensor(f"p5_s{i}", [128, 1024], F32))
              for i in range(2)]
        pden = est.enter_context(nc.psum_tensor("p5_pden", [128, 512], F32))
        pOs = [est.enter_context(nc.psum_tensor(f"p5_pO{i}", [128, 512], F32))
               for i in range(2)]
        pX = est.enter_context(nc.psum_tensor("p5_pX", [128, 512], F32))

        lwo = p.dma("g", lambda e: e.dma_start(out=wo[:], in_=b.d["WoT"],
                                               max_dma_last_dim=8192))
        lkt = p.dma("y", lambda e: e.dma_start(out=kts[:], in_=b.d["kT_s"]))
        lvt = p.dma("y", lambda e: e.dma_start(out=vts[:], in_=b.d["vTok_s"]))
        lmk = p.dma("y", lambda e: e.dma_start(out=msk[:], in_=b.d["masks"]))
        lq_tok = None
        pS_free = [None, None]
        epar_free = [None, None]
        pden_free = None
        pO_free = [None, None]
        pX_free = None
        dbc_free = None
        y_free = None
        sbo_free = None
        st_s = None
        proj_free = [None, None]
        gctr = 0

        def emit_S(h, g0, s, lq):
            nonlocal gctr
            g = h // NKV
            par = gctr % 2
            gctr += 1
            psb = pS[par]
            tpar = t1b[:, par * 2 * SUP:(par + 1) * 2 * SUP]
            epar = expb[:, par * 2 * SUP:(par + 1) * 2 * SUP]
            last_smm = None
            for ii, kb in enumerate((g0, g0 + 1)):
                last_smm = p.t((lambda kkb, iii, psbb, hh, gg: lambda e: e.matmul(
                    psbb[:, iii * 512:(iii + 1) * 512],
                    kts[:, gg * T + kkb * 128: gg * T + (kkb + 1) * 128],
                    qts[:, hh * SUP:(hh + 1) * SUP],
                    start=True, stop=True))(kb, ii, psb, h, g),
                    deps=([lq, lkt, pS_free[par]] if ii == 0 else ()), ms=(ii == 1))
            tt = p.s((lambda psbb, tp2: lambda e: e.activation(
                tp2[:], psbb[:], AF.Tanh, scale=sc_tanh))(psb, tpar),
                deps=[last_smm], ms=True)
            pS_free[par] = tt
            tmsk = None
            for ii, kb in enumerate((g0, g0 + 1)):
                if kb >= 4 * s:
                    m = kb - 4 * s
                    tmsk = p.v((lambda iii, mm, tp2: lambda e: e.tensor_tensor(
                        tp2[:, iii * SUP:(iii + 1) * SUP],
                        tp2[:, iii * SUP:(iii + 1) * SUP],
                        msk[:, mm * SUP:(mm + 1) * SUP], op=ALU.add))(ii, m, tpar),
                        deps=[tt, lmk], ms=True)
            tex = p.s((lambda ep, tp2: lambda e: e.activation(
                ep[:], tp2[:], AF.Exp, scale=SOFT_CAP,
                bias=b.cc[-25.0]))(epar, tpar),
                deps=[tt, tmsk, epar_free[par]], ms=True)
            return (h, g0, par, epar, tex)

        def emit_denav(st, s):
            nonlocal pden_free, epar_free
            h, g0, par, epar, tex = st
            g = h // NKV
            pO = pOs[h % 2]
            nkb = 4 * s + 4
            last_av = None
            for ii, kb in enumerate((g0, g0 + 1)):
                p.t((lambda iii, ep, kkb: lambda e: e.matmul(
                    pden[0:1, :], b.sb["ones_col"][:],
                    ep[:, iii * SUP:(iii + 1) * SUP],
                    start=(kkb == 0), stop=(kkb == nkb - 1)))(ii, epar, kb),
                    deps=([tex, pden_free] if kb == 0 else [tex]))
                last_av = p.t((lambda iii, ep, kkb, gg, pOO: lambda e: e.matmul(
                    pOO[:], vts[:, kkb * 512 + gg * 128: kkb * 512 + (gg + 1) * 128],
                    ep[:, iii * SUP:(iii + 1) * SUP],
                    start=(kkb == 0), stop=(kkb == nkb - 1)))(ii, epar, kb, g, pO),
                    deps=([lvt, pO_free[h % 2]] if kb == 0 else ()), ms=True)
            epar_free[par] = last_av
            return last_av

        def emit_finale(h, last_av):
            nonlocal pden_free, pX_free, dbc_free, y_free
            td = p.s(lambda e: e.activation(denr[0:1, :], pden[0:1, :], AF.Copy),
                     deps=[last_av], ms=True)
            pden_free = td
            tb2 = p.t(lambda e: e.matmul(pX[:], b.sb["ones_row"][:], denr[0:1, :],
                                         start=True, stop=True),
                      deps=[td, pX_free], ms=True)
            tr = p.v(lambda e: e.reciprocal(dbc[:], pX[:]),
                     deps=[tb2, dbc_free], ms=True)
            pX_free = tr
            ty = p.v((lambda hh, pOO: lambda e: e.tensor_tensor(
                ybuf[:, hh * SUP:(hh + 1) * SUP], pOO[:], dbc[:], op=ALU.mult))(h, pOs[h % 2]),
                deps=[last_av, tr, y_free], ms=True)
            pO_free[h % 2] = ty
            dbc_free = ty
            return ty

        for s in range(NSUP):
            lq = p.dma("y", (lambda ss: lambda e: e.dma_start(
                out=qts[:], in_=b.d["qT_s"][ss]))(s), deps=lq_tok or [])
            nkb = 4 * s + 4
            stages = [(h, g0) for h in range(NHEAD) for g0 in range(0, nkb, 2)]
            pend = None
            y_last = None
            for stg in stages + [None]:
                cur = emit_S(stg[0], stg[1], s, lq) if stg is not None else None
                if pend is not None:
                    la = emit_denav(pend, s)
                    if pend[1] + 2 >= nkb:
                        y_last = emit_finale(pend[0], la)
                pend = cur
            lq_tok = [pS_free[0], pS_free[1]]
            pp = [pX, pden]
            st_deps = None
            tm = None
            for ot in range(NDT):
                pk = pp[ot % 2]
                pfree = [pX_free, pden_free][ot % 2] if ot < 2 else proj_free[ot % 2]
                for j in range(NDT):
                    tm = p.t((lambda jj, oo, pkk: lambda e: e.matmul(
                        pkk[:], wo[:, jj * D + oo * 128: jj * D + (oo + 1) * 128],
                        ybuf[:, jj * SUP:(jj + 1) * SUP],
                        start=(jj == 0), stop=(jj == NDT - 1)))(j, ot, pk),
                        deps=([lwo, y_last, pfree] if j == 0 else ()),
                        ms=(j == NDT - 1))
                tev = p.s((lambda oo, pkk: lambda e: e.activation(
                    sbo[:, oo * SUP:(oo + 1) * SUP], pkk[:], AF.Identity,
                    bias=b.sb["bo"][:, oo:oo + 1]))(ot, pk),
                    deps=[tm] + ([sbo_free] if ot == 0 else []), ms=True)
                proj_free[ot % 2] = tev
                st_deps = tev
            pX_free = proj_free[0]
            pden_free = proj_free[1]
            y_free = tm
            st_s = p.dma("s", (lambda ss: lambda e: e.dma_start(
                out=b.d["sT1_s"][ss], in_=sbo[:]))(s), deps=[st_deps])
            sbo_free = st_s
    p.barrier()


def emit_P6(b: Build, ctx):
    nc, p = b.nc, b.p
    from contextlib import ExitStack
    with ExitStack() as est:
        def al(name, shape, dt):
            return est.enter_context(nc.sbuf_tensor(name, shape, dt))
        wg2b = al("p6_wg", [128, 2 * NDT * 1024], BF16)
        wu2b = al("p6_wu", [128, 2 * NDT * 1024], BF16)
        xnb2 = al("p6_xn", [128, 2 * NDT * SUP], BF16)
        gbuf = al("p6_g", [128, SUP], BF16)
        ubuf = al("p6_u", [128, SUP], BF16)
        wdstg = al("p6_wdstg", [128, 64 * 128], BF16)
        fbuf = al("p6_f", [128, 8 * SUP], BF16)
        pg = [est.enter_context(nc.psum_tensor(f"p6_pg{i}", [128, 512], F32)) for i in range(2)]
        pu = [est.enter_context(nc.psum_tensor(f"p6_pu{i}", [128, 512], F32)) for i in range(2)]
        W1 = NDT * 1024
        wfree = [None, None]
        lw_toks = [None, None]
        wd_cv_st = None
        xn_use = [None, None]
        pg_free = [None, None]
        pu_free = [None, None]
        gb_free = None
        ub_free = None
        fb_free = None
        st_f = None
        for ib in range(8):
            par = ib % 2
            wg = wg2b[:, par * W1:(par + 1) * W1]
            wu = wu2b[:, par * W1:(par + 1) * W1]
            lg2 = p.dma("g", (lambda ibb, dst: lambda e: e.dma_start(
                out=dst, in_=b.d["w_gate"][ibb], max_dma_last_dim=2048))(ib, wg),
                deps=[wfree[par]])
            lu2 = p.dma("g", (lambda ibb, dst: lambda e: e.dma_start(
                out=dst, in_=b.d["w_up"][ibb], max_dma_last_dim=2048))(ib, wu),
                deps=[wfree[par]])
            lw_toks[par] = [lg2, lu2]
            for ht in (2 * ib, 2 * ib + 1):
                cv_ld = p.dma("g", (lambda hh: lambda e: e.dma_start(
                    out=wdstg[:], in_=b.d["w_down"][hh], max_dma_last_dim=2048))(ht),
                    deps=[wd_cv_st])
                wd_cv_st = p.dma("s", (lambda hh: lambda e: e.dma_start(
                    out=b.d["wd_bf_s"][hh], in_=wdstg[:]))(ht), deps=[cv_ld])
            last_mm = None
            for s in range(NSUP):
                xnb = xnb2[:, (s % 2) * NDT * SUP:(s % 2 + 1) * NDT * SUP]
                lxn = p.dma("y", (lambda ss, dst: lambda e: e.dma_start(
                    out=dst, in_=b.d["xnT1_s"][ss]))(s, xnb), deps=[xn_use[s % 2]])
                tf_last = None
                for it in range(8):
                    pkg = pg[it % 2]
                    pku = pu[it % 2]
                    tmg = tmu = None
                    for j in range(NDT):
                        tmg = p.t((lambda jj, itt, pkk, wgg, xx: lambda e: e.matmul(
                            pkk[:], wgg[:, jj * 1024 + itt * 128: jj * 1024 + (itt + 1) * 128],
                            xx[:, jj * SUP:(jj + 1) * SUP],
                            start=(jj == 0), stop=(jj == NDT - 1)))(j, it, pkg, wg, xnb),
                            deps=([lg2, lxn, pg_free[it % 2]] if j == 0 else ()),
                            ms=(j == NDT - 1))
                    for j in range(NDT):
                        tmu = p.t((lambda jj, itt, pkk, wuu, xx: lambda e: e.matmul(
                            pkk[:], wuu[:, jj * 1024 + itt * 128: jj * 1024 + (itt + 1) * 128],
                            xx[:, jj * SUP:(jj + 1) * SUP],
                            start=(jj == 0), stop=(jj == NDT - 1)))(j, it, pku, wu, xnb),
                            deps=([lu2, lxn, pu_free[it % 2]] if j == 0 else ()),
                            ms=(j == NDT - 1))
                    last_mm = tmu
                    ic = ib * 8 + it
                    tg2 = p.s((lambda icc, pkk: lambda e: e.activation(
                        gbuf[:], pkk[:], AF.Silu,
                        scale=b.sb["s_gate"][:, icc:icc + 1]))(ic, pkg),
                        deps=[tmg, gb_free], ms=True)
                    pg_free[it % 2] = tg2
                    tu2 = p.s((lambda icc, pkk: lambda e: e.activation(
                        ubuf[:], pkk[:], AF.Identity,
                        scale=b.sb["s_up"][:, icc:icc + 1]))(ic, pku),
                        deps=[tmu, ub_free], ms=True)
                    pu_free[it % 2] = tu2
                    tf_last = p.v((lambda itt: lambda e: e.tensor_tensor(
                        fbuf[:, itt * SUP:(itt + 1) * SUP], gbuf[:], ubuf[:],
                        op=ALU.mult))(it),
                        deps=[tg2, tu2] + ([st_f] if it == 0 else []), ms=True)
                    gb_free = tf_last
                    ub_free = tf_last
                xn_use[s % 2] = last_mm
                st_f = p.dma("s", (lambda ss, ibb: lambda e: e.dma_start(
                    out=b.d["fuseT_s"][ss][:, ibb * 8 * SUP:(ibb + 1) * 8 * SUP],
                    in_=fbuf[:]))(s, ib), deps=[tf_last])
            wfree[par] = last_mm
    p.barrier()


def emit_P7(b: Build, ctx):
    nc, p = b.nc, b.p
    from contextlib import ExitStack
    with ExitStack() as est:
        def al(name, shape, dt):
            return est.enter_context(nc.sbuf_tensor(name, shape, dt))
        fb2 = al("p7_f", [128, 64 * SUP], BF16)
        wd2 = al("p7_wd", [128, 2 * 64 * 128], BF16)
        s1b = al("p7_s1", [128, NDT * SUP], BF16)
        xfb = al("p7_x", [128, NDT * SUP], F32)
        gti = al("p7_g", [128, SUP], BF16)
        pti = al("p7_p", [128, SUP], BF16)
        pp = [est.enter_context(nc.psum_tensor(f"p7_p{i}", [128, 512], F32)) for i in range(2)]
        WD = 64 * 128
        use = []
        wd_free = [None, None]
        pp_free = [None, None]
        g_free = None
        p_free = None
        st_y = None
        fb_use = [None, None]
        HF = 32 * SUP
        for s in range(NSUP):
            fb = fb2[:, :]
            lfA = p.dma("y", (lambda ss, dst: lambda e: e.dma_start(
                out=dst, in_=b.d["fuseT_s"][ss][:, 0:HF]))(s, fb2[:, 0:HF]),
                deps=[fb_use[0]])
            lfB = p.dma("y", (lambda ss, dst: lambda e: e.dma_start(
                out=dst, in_=b.d["fuseT_s"][ss][:, HF:2 * HF]))(s, fb2[:, HF:2 * HF]),
                deps=[fb_use[0]])
            ls1 = p.dma("y", (lambda ss: lambda e: e.dma_start(
                out=s1b[:], in_=b.d["sT1_s"][ss]))(s), deps=use)
            lxf = p.dma("y", (lambda ss: lambda e: e.dma_start(
                out=xfb[:], in_=b.d["xT1_s"][ss]))(s), deps=use)
            x_done = None
            last_mm = None
            for ht in range(NDT):
                idx = s * NDT + ht
                par = idx % 2
                wd = wd2[:, par * WD:(par + 1) * WD]
                lwd = p.dma("g", (lambda hh, dst: lambda e: e.dma_start(
                    out=dst, in_=b.d["wd_bf_s"][hh]))(ht, wd),
                    deps=[wd_free[par]])
                pk = pp[ht % 2]
                tm = None
                for it in range(64):
                    dep0 = []
                    if it == 0:
                        dep0 = [lwd, lfA, pp_free[ht % 2]]
                    elif it == 32:
                        dep0 = [lfB]
                    tm = p.t((lambda itt, pkk, wdd, ff: lambda e: e.matmul(
                        pkk[:], wdd[:, itt * 128:(itt + 1) * 128],
                        ff[:, itt * SUP:(itt + 1) * SUP],
                        start=(itt == 0), stop=(itt == 63)))(it, pk, wd, fb),
                        deps=dep0, ms=(it == 63))
                wd_free[par] = tm
                last_mm = tm
                tg = p.s((lambda hh, pkk: lambda e: e.activation(
                    gti[:], pkk[:], AF.Silu,
                    scale=b.sb["s_down"][:, hh:hh + 1]))(ht, pk),
                    deps=[tm, g_free], ms=True)
                pp_free[ht % 2] = tg
                tp = p.v((lambda hh: lambda e: e.tensor_tensor(
                    pti[:], gti[:], s1b[:, hh * SUP:(hh + 1) * SUP], op=ALU.mult))(ht),
                    deps=[tg, ls1, p_free], ms=True)
                g_free = tp
                x_done = p.v((lambda hh: lambda e: e.tensor_tensor(
                    xfb[:, hh * SUP:(hh + 1) * SUP], xfb[:, hh * SUP:(hh + 1) * SUP],
                    pti[:], op=ALU.add))(ht), deps=[tp, lxf], ms=True)
                p_free = x_done
            st_y = p.dma("s", (lambda ss: lambda e: e.dma_start(
                out=b.d["y_out"][ss], in_=xfb[:]))(s), deps=[x_done])
            fb_use[0] = last_mm
            use = [x_done, st_y]
    p.barrier()


PHASES = {
    "P0": emit_P0,
    "P1": emit_P1,
    "P2a": emit_P2a,
    "P2b": emit_P2b,
    "P3": emit_P3,
    "P4": emit_P4,
    "P5": emit_P5,
    "P6": emit_P6,
    "P7": emit_P7,
}


def build_program(phases=("P0", "P1", "P2", "P3"), dbg=()):
    from contextlib import ExitStack
    b = Build(dbg=dbg)
    declare_tensors(b)
    with ExitStack() as ctx:
        alloc_persistent(b, ctx)
        b.p.barrier()
        for ph in phases:
            PHASES[ph](b, ctx)
        finalize(b)
    return b


def finalize(b: Build):
    nc = b.nc
    p = b.p
    from contextlib import ExitStack
    with ExitStack() as sctx:
        sems = {s: sctx.enter_context(nc.semaphore(f"sem_{s}")) for s in SEMS}
        with nc.Block() as block:
            def playback(stream, eng):
                for ent in stream:
                    if ent[0] == "w":
                        eng.wait_ge(sems[ent[1]], ent[2])
                    else:
                        inst = ent[1](eng)
                        for (sname, n) in ent[2]:
                            inst.then_inc(sems[sname], n)

            @block.tensor
            def _(eng):
                playback(p.streams["t"], eng)

            @block.scalar
            def _(eng):
                playback(p.streams["s"], eng)

            @block.vector
            def _(eng):
                playback(p.streams["v"], eng)

            @block.sync
            def _(eng):
                playback(p.streams["y"], eng)

            @block.gpsimd
            def _(eng):
                playback(p.streams["g"], eng)
                eng.wait_ge(sems["SD"], p.counts["SD"])
                eng.wait_ge(sems["Y"], p.counts["Y"])



def prep_inputs(**inp):
    f = {k: np.asarray(v) for k, v in inp.items()}
    cosT, sinT = _rope_tables()
    masks = _build_masks()
    shared = {
        "WuT": _sbufify(np.ascontiguousarray(f["L0_attn_Wu"].T)),
        "WvT": _sbufify(np.ascontiguousarray(f["L0_attn_Wv"].T)),
        "Wg1T": _sbufify(np.ascontiguousarray(f["L0_attn_Wg1"].T)),
        "Wg2T": _sbufify(np.ascontiguousarray(f["L0_attn_Wg2"].T)),
        "w0T": _sbufify(np.ascontiguousarray(f["L0_ffn_weight"].T)),
        "bu": _pcol(f["L0_attn_bu"]), "bv": _pcol(f["L0_attn_bv"]),
        "bg1": _pcol(f["L0_attn_bg1"]), "bg2": _pcol(f["L0_attn_bg2"]),
        "convw": np.ascontiguousarray(
            f["L0_attn_conv_w"].reshape(D, 3).reshape(NDT, 128, 3)
            .transpose(1, 0, 2).reshape(128, 48)),
        "convb": _pcol(f["L0_attn_conv_b"]),
        "b0": _pcol(f["L0_ffn_bias"]),
        "WqT": _sbufify(np.ascontiguousarray(f["L1_attn_Wq"].T)),
        "WkT": _sbufify(np.ascontiguousarray(f["L1_attn_Wk"].T)),
        "WvT1": _sbufify(np.ascontiguousarray(f["L1_attn_Wv"].T)),
        "WoT": _sbufify(np.ascontiguousarray(f["L1_attn_Wo"].T)),
        "bo": _pcol(f["L1_attn_bo"]),
        "qnw": _pcol(f["L1_attn_q_norm_w"]), "knw": _pcol(f["L1_attn_k_norm_w"]),
        "w_gate": np.ascontiguousarray(
            f["L1_ffn_w_gate"].reshape(NDT, 128, 8, 1024).transpose(2, 1, 0, 3)
            .reshape(8, 128, NDT * 1024)),
        "w_up": np.ascontiguousarray(
            f["L1_ffn_w_up"].reshape(NDT, 128, 8, 1024).transpose(2, 1, 0, 3)
            .reshape(8, 128, NDT * 1024)),
        "w_down": np.ascontiguousarray(
            f["L1_ffn_w_down"].reshape(64, 128, NDT, 128).transpose(2, 1, 0, 3)
            .reshape(NDT, 128, 64 * 128)),
        "s_gate": np.ascontiguousarray(f["L1_ffn_s_gate"].reshape(64, 128).T),
        "s_up": np.ascontiguousarray(f["L1_ffn_s_up"].reshape(64, 128).T),
        "s_down": _pcol(f["L1_ffn_s_down"]),
        "ones_col": np.ones((128, 1), dtype=BF),
        "ones_row": np.ones((1, 128), dtype=BF),
        "perm128": np.eye(128, dtype=BF)[:, list(range(64, 128)) + list(range(64))],
        "cosT": cosT, "sinT": sinT, "masks": masks,
    }
    in_maps = []
    x = f["L0_attn_x"]
    for c in range(8):
        m = dict(shared)
        m["x"] = _xfm_x(np.asarray(x[c]))
        in_maps.append(m)
    return in_maps


def kernel(**inp):
    in_maps = prep_inputs(**inp)
    b = build_program(phases=tuple(PHASES.keys()))
    res = run_bass_kernel_spmd(b.nc, in_maps, core_ids=list(range(8)))
    out = np.empty((8, T, D), dtype=np.float32)
    for c in range(8):
        out[c] = _inv_x(np.asarray(res.results[c]["y_out"], dtype=np.float32))
    return out
